# version 13
# baseline (speedup 1.0000x reference)
import contextlib
import numpy as np
import concourse.bass as bass
import concourse.mybir as mybir
from concourse.bass_utils import run_bass_kernel_spmd

F32 = mybir.dt.float32
BF16 = mybir.dt.bfloat16
AF = mybir.ActivationFunctionType
ALU = mybir.AluOpType
AX = mybir.AxisListType

PARANOID = False
SEQ = 4096
D = 1024
NT = SEQ // 128
NCH = D // 128
NTB = SEQ // 512
LN_EPS = 1e-5


class Ctx:

    ENGS = ("t", "v", "s", "g", "y")

    def __init__(self, nc, stack):
        self.nc = nc
        self.eng = {
            "t": nc.tensor, "v": nc.vector, "s": nc.scalar,
            "g": nc.gpsimd, "y": nc.sync,
        }
        self.sems = {}
        self.cnt = {}
        self.waited = {}
        for name in ("T", "V", "S", "G", "DY", "DG", "DS", "B"):
            self.sems[name] = stack.enter_context(nc.semaphore("sem" + name))
            self.cnt[name] = 0

    def setup_consts(self, st):
        nc = self.nc
        self.cconst = st.enter_context(nc.sbuf_tensor("cc_pool", [128, 4], F32))
        nc.vector.memset(self.cconst[:, 0:1], LN_EPS)
        nc.vector.memset(self.cconst[:, 1:2], 0.0)
        nc.vector.memset(self.cconst[:, 2:3], 1.0).then_inc(self.sems["V"], 1)
        self.const_evt = self.inc("V", 1)
        self.eps_col = self.cconst[:, 0:1]

    def inc(self, sem, k):
        self.cnt[sem] += k
        return (sem, self.cnt[sem])

    def wait(self, ename, evt):
        if evt is None:
            return
        if isinstance(evt, (list,)):
            for e in evt:
                self.wait(ename, e)
            return
        sem, c = evt
        if c <= 0:
            return
        key = (ename, sem)
        if self.waited.get(key, 0) >= c:
            return
        self.eng[ename].wait_ge(self.sems[sem], c)
        self.waited[key] = c

    def barrier(self):
        self.wait("y", ("DY", self.cnt["DY"]))
        self.wait("g", ("DG", self.cnt["DG"]))
        self.wait("s", ("DS", self.cnt["DS"]))
        self.cnt["B"] += len(self.ENGS)
        tgt = self.cnt["B"]
        for en in self.ENGS:
            self.eng[en].sem_inc(self.sems["B"], 1)
        for en in self.ENGS:
            self.eng[en].wait_ge(self.sems["B"], tgt)
            self.waited[(en, "B")] = tgt


def emit_phase_a(ctx, st, x_hbm, xn_dim, ident_evt, ident_bf, xn_tok=None,
                 stats_sb=None, ring=2, tr_banks=2):
    nc = ctx.nc
    xin = [st.enter_context(nc.sbuf_tensor(f"pa_x{i}", [128, 1024], F32)) for i in range(ring)]
    own_tok = xn_tok is None
    if own_tok:
        xn_tok = [st.enter_context(nc.sbuf_tensor(f"pa_n{i}", [128, 1024], BF16)) for i in range(ring)]
    bn12 = st.enter_context(nc.sbuf_tensor("pa_bn", [128, 12], F32))
    stats = stats_sb if stats_sb is not None else st.enter_context(
        nc.sbuf_tensor("pa_st", [128, 4 * NT], F32))
    pstr = [st.enter_context(nc.psum_tensor(f"pa_tr{i}", [128, 128], BF16)) for i in range(tr_banks)]

    xin_free = [None] * ring
    tok_free = [None] * ring
    tr_free = [None, None]
    col_done = [None] * NT

    for i in range(NT):
        sl = i % ring
        ctx.wait("y", xin_free[sl])
        nc.sync.dma_start(out=xin[sl][:], in_=x_hbm[i * 128:(i + 1) * 128, :]
                          ).then_inc(ctx.sems["DY"], 16)
        ev_load = ctx.inc("DY", 16)

        ctx.wait("v", ev_load)
        nc.vector.bn_stats(bn12[:, 0:6], xin[sl][:, 0:512])
        nc.vector.bn_stats(bn12[:, 6:12], xin[sl][:, 512:1024])
        nc.vector.drain()
        nc.vector.bn_aggr(stats[:, 4 * i:4 * i + 2], bn12[:]).then_inc(ctx.sems["V"], 1)
        ev_mv = ctx.inc("V", 1)

        ctx.wait("s", ev_mv)
        ctx.wait("s", ctx.const_evt)
        nc.scalar.activation(out=stats[:, 4 * i + 2:4 * i + 3],
                             in_=stats[:, 4 * i + 1:4 * i + 2], func=AF.Ln,
                             bias=ctx.eps_col, scale=1.0)
        nc.scalar.drain()
        nc.scalar.activation(out=stats[:, 4 * i + 2:4 * i + 3],
                             in_=stats[:, 4 * i + 2:4 * i + 3], func=AF.Exp,
                             scale=-0.5).then_inc(ctx.sems["S"], 1)
        ev_rstd = ctx.inc("S", 1)

        ctx.wait("v", ev_rstd)
        nc.vector.tensor_scalar(out=stats[:, 4 * i + 3:4 * i + 4],
                                in0=stats[:, 4 * i:4 * i + 1],
                                scalar1=stats[:, 4 * i + 2:4 * i + 3], scalar2=-1.0,
                                op0=ALU.mult, op1=ALU.mult)
        nc.vector.drain()
        ctx.wait("v", tok_free[sl])
        dst_tok = xn_tok[sl] if own_tok else xn_tok[i]
        nc.vector.tensor_scalar(out=dst_tok[:], in0=xin[sl][:],
                                scalar1=stats[:, 4 * i + 2:4 * i + 3],
                                scalar2=stats[:, 4 * i + 3:4 * i + 4],
                                op0=ALU.mult, op1=ALU.add).then_inc(ctx.sems["V"], 1)
        ev_xn = ctx.inc("V", 1)
        xin_free[sl] = ev_xn

        ev_last_v = None
        ev_last_s = None
        for c in range(NCH):
            pb = c % tr_banks
            ctx.wait("t", ev_xn)
            ctx.wait("t", ident_evt)
            ctx.wait("t", tr_free[pb])
            nc.tensor.transpose(pstr[pb][:], dst_tok[:, c * 128:(c + 1) * 128],
                                ident_bf[:]).then_inc(ctx.sems["T"], 1)
            ev_tr = ctx.inc("T", 1)
            if pb == 0:
                ctx.wait("v", ev_tr)
                nc.vector.tensor_copy(xn_dim[c][:, i * 128:(i + 1) * 128], pstr[pb][:]
                                      ).then_inc(ctx.sems["V"], 1)
                ev_evac = ctx.inc("V", 1)
                ev_last_v = ev_evac
            else:
                ctx.wait("s", ev_tr)
                nc.scalar.copy(xn_dim[c][:, i * 128:(i + 1) * 128], pstr[pb][:]
                               ).then_inc(ctx.sems["S"], 1)
                ev_evac = ctx.inc("S", 1)
                ev_last_s = ev_evac
            tr_free[pb] = ev_evac
        col_done[i] = [ev_last_v, ev_last_s]
        tok_free[sl] = ev_tr

    return {"col_done": col_done, "stats": stats}


def mk_phase_c_bufs(ctx, st):
    nc = ctx.nc
    return {
        "xrel": [st.enter_context(nc.sbuf_tensor(f"pc_xr{i}", [128, 1024], F32)) for i in range(2)],
        "bn12c": st.enter_context(nc.sbuf_tensor("pc_bn", [128, 12], F32)),
        "cstats": st.enter_context(nc.sbuf_tensor("pc_st", [128, 8], F32)),
        "xrel_free": [None] * 2,
        "cst_free": [None, None],
    }


def emit_phase_c(ctx, st, x_hbm_in, x_hbm_out, y_psum, y_evt, i, bufs):
    nc = ctx.nc
    xrel = bufs["xrel"]
    bn12, cst = bufs["bn12c"], bufs["cstats"]
    sl = i % 2
    r = i % 2
    co = 4 * r
    ctx.wait("y", bufs["xrel_free"][sl])
    nc.sync.dma_start(out=xrel[sl][:], in_=x_hbm_in[i * 128:(i + 1) * 128, :]
                      ).then_inc(ctx.sems["DY"], 16)
    ev_xload = ctx.inc("DY", 16)

    ctx.wait("v", y_evt)
    nc.vector.bn_stats(bn12[:, 0:6], y_psum[:, 0:512])
    nc.vector.bn_stats(bn12[:, 6:12], y_psum[:, 512:1024])
    nc.vector.drain()
    ctx.wait("v", bufs["cst_free"][r])
    nc.vector.bn_aggr(cst[:, co:co + 2], bn12[:]).then_inc(ctx.sems["V"], 1)
    ev_mv = ctx.inc("V", 1)

    ctx.wait("s", ev_mv)
    ctx.wait("s", ctx.const_evt)
    nc.scalar.activation(out=cst[:, co + 2:co + 3], in_=cst[:, co + 1:co + 2],
                         func=AF.Ln, bias=ctx.eps_col, scale=1.0)
    nc.scalar.drain()
    nc.scalar.activation(out=cst[:, co + 2:co + 3], in_=cst[:, co + 2:co + 3],
                         func=AF.Exp, scale=-0.5).then_inc(ctx.sems["S"], 1)
    ev_rstd = ctx.inc("S", 1)
    bufs["cst_free"][r] = ev_rstd

    ctx.wait("v", ev_rstd)
    nc.vector.tensor_scalar(out=cst[:, co + 3:co + 4], in0=cst[:, co:co + 1],
                            scalar1=cst[:, co + 2:co + 3], scalar2=-1.0,
                            op0=ALU.mult, op1=ALU.mult).then_inc(ctx.sems["V"], 1)
    ev_nmr = ctx.inc("V", 1)
    ctx.wait("s", ev_nmr)
    nc.scalar.activation(out=y_psum[:], in_=y_psum[:], func=AF.Identity,
                         scale=cst[:, co + 2:co + 3], bias=cst[:, co + 3:co + 4]
                         ).then_inc(ctx.sems["S"], 1)
    ev_yn = ctx.inc("S", 1)
    ctx.wait("v", ev_yn)
    ctx.wait("v", ev_xload)
    nc.vector.tensor_tensor(out=xrel[sl][:], in0=xrel[sl][:],
                            in1=y_psum[:], op=ALU.add).then_inc(ctx.sems["V"], 1)
    ev_x = ctx.inc("V", 1)

    ctx.wait("y", ev_x)
    nc.sync.dma_start(out=x_hbm_out[i * 128:(i + 1) * 128, :], in_=xrel[sl][:]
                      ).then_inc(ctx.sems["DY"], 16)
    ev_done = ctx.inc("DY", 16)
    bufs["xrel_free"][sl] = ev_done
    return ev_done, ev_x


def emit_l0_ffn(ctx, st, io, W):
    nc = ctx.nc
    xn_dim = [st.enter_context(nc.sbuf_tensor(f"f_xnd{c}", [128, 4096], BF16)) for c in range(NCH)]
    ident_bf = st.enter_context(nc.sbuf_tensor("f_id", [128, 128], BF16))
    winrep = [st.enter_context(nc.sbuf_tensor(f"f_wr{c}", [128, 1024], BF16)) for c in range(NCH)]
    hb = st.enter_context(nc.sbuf_tensor("f_hb", [128, 3 * NCH], F32))
    hbias = st.enter_context(nc.sbuf_tensor("f_hbias", [128, NCH], F32))
    w2bd = [st.enter_context(nc.sbuf_tensor(f"f_w2{c}", [128, 64], BF16)) for c in range(NCH)]
    b2c = st.enter_context(nc.sbuf_tensor("f_b2", [64, 1], F32))
    wout = st.enter_context(nc.sbuf_tensor("f_wo", [64, 1024], BF16))
    bout = st.enter_context(nc.sbuf_tensor("f_bo", [1, 1024], BF16))
    onesc = st.enter_context(nc.sbuf_tensor("f_ones", [1, 128], BF16))
    hsb = [st.enter_context(nc.sbuf_tensor(f"f_h{i}", [128, 512], BF16)) for i in range(2)]
    gsb = [st.enter_context(nc.sbuf_tensor(f"f_g{i}", [64, 512], BF16)) for i in range(2)]

    def gdma(dst, src):
        nc.gpsimd.dma_start(out=dst, in_=src).then_inc(ctx.sems["DG"], 16)
        return ctx.inc("DG", 16)

    ev_id = gdma(ident_bf[:], W["ident"][:, :])
    for c in range(NCH):
        ev_wr = gdma(winrep[c][:], W["winrep"][c * 128:(c + 1) * 128, :])
    for c in range(NCH):
        gdma(hb[:, 3 * c:3 * c + 1], W["binrep"][c * 128:(c + 1) * 128, :])
        gdma(hb[:, 3 * c + 1:3 * c + 2], W["w1f"][c * 128:(c + 1) * 128, :])
        ev_hb = gdma(hb[:, 3 * c + 2:3 * c + 3], W["b1f"][c * 128:(c + 1) * 128, :])
    for c in range(NCH):
        ev_w2 = gdma(w2bd[c][:], W["w2bd"][c, :, :])
    gdma(b2c[:], W["b2"][:, :])
    gdma(wout[:], W["wout"][:, :])
    ev_wo = gdma(bout[:], W["boutrow"][:, :])

    ctx.wait("v", ev_hb)
    for c in range(NCH):
        nc.vector.tensor_scalar(out=hbias[:, c:c + 1], in0=hb[:, 3 * c:3 * c + 1],
                                scalar1=hb[:, 3 * c + 1:3 * c + 2],
                                scalar2=hb[:, 3 * c + 2:3 * c + 3],
                                op0=ALU.mult, op1=ALU.add)
    nc.vector.memset(onesc[:], 1.0).then_inc(ctx.sems["V"], 1)
    ev_hbias = ctx.inc("V", 1)

    pa = emit_phase_a(ctx, st, io["x_in"], xn_dim, ev_id, ident_bf, ring=4, tr_banks=1)

    ps_s = [st.enter_context(nc.psum_tensor(f"f_ps{i}", [128, 512], F32)) for i in range(2)]
    ps_g = st.enter_context(nc.psum_tensor("f_pg", [64, 512], F32))
    ps_y = [st.enter_context(nc.psum_tensor(f"f_py{i}", [128, 1024], F32)) for i in range(2)]
    cbufs = mk_phase_c_bufs(ctx, st)

    ps_s_free = [None, None]
    ps_g_free = None
    ps_y_free = [None, None]
    h_free = [None, None]
    g_free = [None, None]
    ev_done = None

    for tb in range(NTB):
        tbs = slice(tb * 512, (tb + 1) * 512)
        ev_cols = pa["col_done"][4 * tb + 3]
        ev_g = None
        for c in range(NCH):
            pb = c % 2
            ctx.wait("t", ev_cols)
            ctx.wait("t", ev_wr)
            ctx.wait("t", ps_s_free[pb])
            for kc in range(NCH):
                mm = nc.tensor.matmul(ps_s[pb][:], winrep[kc][:, c * 128:(c + 1) * 128],
                                      xn_dim[kc][:, tbs], start=(kc == 0),
                                      stop=(kc == NCH - 1), skip_group_check=True)
            mm.then_inc(ctx.sems["T"], 1)
            ev_srep = ctx.inc("T", 1)
            ctx.wait("s", ev_srep)
            ctx.wait("s", ev_hbias)
            ctx.wait("s", h_free[pb])
            nc.scalar.activation(out=hsb[pb][:], in_=ps_s[pb][:], func=AF.Relu,
                                 scale=hb[:, 3 * c + 1:3 * c + 2],
                                 bias=hbias[:, c:c + 1]).then_inc(ctx.sems["S"], 1)
            ev_h = ctx.inc("S", 1)
            ps_s_free[pb] = ev_h
            ctx.wait("t", ev_h)
            ctx.wait("t", ev_w2)
            if c == 0:
                ctx.wait("t", ps_g_free)
            nc.tensor.matmul(ps_g[:], w2bd[c][:], hsb[pb][:], start=(c == 0),
                             stop=(c == NCH - 1), skip_group_check=True
                             ).then_inc(ctx.sems["T"], 1)
            ev_g = ctx.inc("T", 1)
            h_free[pb] = ev_g
        gb = tb % 2
        ctx.wait("v", ev_g)
        ctx.wait("v", g_free[gb])
        nc.vector.tensor_scalar(out=gsb[gb][:], in0=ps_g[:], scalar1=b2c[:],
                                scalar2=None, op0=ALU.add).then_inc(ctx.sems["V"], 1)
        ev_gt = ctx.inc("V", 1)
        ps_g_free = ev_gt

        for u in range(4):
            i = 4 * tb + u
            yr = i % 2
            ctx.wait("t", ev_gt)
            ctx.wait("t", ev_wo)
            ctx.wait("t", ps_y_free[yr])
            for half in range(2):
                nc.tensor.matmul(ps_y[yr][:, half * 512:(half + 1) * 512],
                                 gsb[gb][:, u * 128:(u + 1) * 128],
                                 wout[:, half * 512:(half + 1) * 512],
                                 start=True, stop=False, skip_group_check=True)
                mm = nc.tensor.matmul(ps_y[yr][:, half * 512:(half + 1) * 512],
                                      onesc[:, 0:128],
                                      bout[:, half * 512:(half + 1) * 512],
                                      start=False, stop=True, skip_group_check=True)
            mm.then_inc(ctx.sems["T"], 1)
            ev_y = ctx.inc("T", 1)
            ev_done, yfree = emit_phase_c(ctx, st, io["x_in"], io["x_out"],
                                          ps_y[yr], ev_y, i, cbufs)
            ps_y_free[yr] = yfree
            if u == 3:
                g_free[gb] = yfree
    return ev_done


def flush_divide(pending, ctx, nc, ps_s, rden, rb_sb, zdiv, z_free,
                 s_free, zdiv_done, ones_r2, ev_ones2):
    h0, z_t, ev_rd = pending[0]
    for p in range(2):
        h = h0 + p
        ctx.wait("t", ev_rd)
        ctx.wait("t", ev_ones2)
        ctx.wait("t", s_free[1])
        nc.tensor.matmul(ps_s[1][:, 512 * p:512 * p + 512], ones_r2[:],
                         rden[0:1, 512 * p:512 * p + 512],
                         start=True, stop=True, skip_group_check=True
                         ).then_inc(ctx.sems["T"], 1)
        ev_bc = ctx.inc("T", 1)
        pending[1] = ev_bc
        ctx.wait("v", ev_bc)
        nc.vector.tensor_copy(rb_sb[:], ps_s[1][:, 512 * p:512 * p + 512])
        nc.vector.drain()
        nc.vector.tensor_tensor(out=zdiv[h][:], in0=z_t[0:64, 512 * p:512 * p + 512],
                                in1=rb_sb[0:64, :], op=ALU.mult
                                ).then_inc(ctx.sems["V"], 1)
        ev_zd = ctx.inc("V", 1)
        z_free[id(z_t)][p] = ev_zd
        s_free[1] = ev_zd
        zdiv_done[h] = ev_zd
    pending[0] = None


def emit_l0_attn(ctx, st, io, W):
    nc = ctx.nc
    NH = 16

    with contextlib.ExitStack() as sp:
        xn_dim = [sp.enter_context(nc.sbuf_tensor(f"o_xnd{c}", [128, 4096], BF16)) for c in range(NCH)]
        ident_bf = sp.enter_context(nc.sbuf_tensor("o_id", [128, 128], BF16))
        wqkv = [sp.enter_context(nc.sbuf_tensor(f"o_wq{c}", [128, 3072], BF16)) for c in range(NCH)]
        bqk = sp.enter_context(nc.sbuf_tensor("o_bqk", [128, 16], F32))
        bvr = sp.enter_context(nc.sbuf_tensor("o_bv", [1, 1024], BF16))
        ones_r = sp.enter_context(nc.sbuf_tensor("o_1r", [1, 128], BF16))
        qk_stage = [sp.enter_context(nc.sbuf_tensor(f"o_qs{i}", [128, 2048], BF16)) for i in range(2)]
        v_stage = [sp.enter_context(nc.sbuf_tensor(f"o_vs{i}", [128, 1056], BF16)) for i in range(2)]

        def gdma(dst, src):
            nc.gpsimd.dma_start(out=dst, in_=src).then_inc(ctx.sems["DG"], 16)
            return ctx.inc("DG", 16)

        ev_id = gdma(ident_bf[:], W["ident"][:, :])
        for c in range(NCH):
            ev_w = gdma(wqkv[c][:], W["wqkv"][c * 128:(c + 1) * 128, :])
        for q in range(2):
            for c in range(NCH):
                ev_b = gdma(bqk[:, 8 * q + c:8 * q + c + 1], W["bqk"][q, c * 128:(c + 1) * 128, :])
        ev_bv = gdma(bvr[:], W["bvrow"][:, :])
        nc.vector.memset(ones_r[:], 1.0).then_inc(ctx.sems["V"], 1)
        ev_ones = ctx.inc("V", 1)

        pa = emit_phase_a(ctx, sp, io["x_in"], xn_dim, ev_id, ident_bf, ring=4)

        ps_p = [sp.enter_context(nc.psum_tensor(f"o_pp{i}", [128, 512], F32)) for i in range(2)]
        ps_v = sp.enter_context(nc.psum_tensor("o_pv", [128, 1024], F32))
        pp_free = [None, None]
        st_free = [None] * 4
        vstg_free = [None, None]
        pv_free = None

        for tb in range(NTB):
            tbs = slice(tb * 512, (tb + 1) * 512)
            ev_cols = pa["col_done"][4 * tb + 3]
            for oc in range(16):
                r = oc % 2
                sl = (oc // 4) % 2
                so = oc % 4
                ctx.wait("t", ev_cols)
                ctx.wait("t", ev_w)
                ctx.wait("t", pp_free[r])
                for kc in range(NCH):
                    mm = nc.tensor.matmul(ps_p[r][:],
                                          wqkv[kc][:, oc * 128:(oc + 1) * 128],
                                          xn_dim[kc][:, tbs], start=(kc == 0),
                                          stop=(kc == NCH - 1), skip_group_check=True)
                mm.then_inc(ctx.sems["T"], 1)
                ev_p = ctx.inc("T", 1)
                ctx.wait("v", ev_p)
                ctx.wait("v", ev_b)
                if so == 0:
                    ctx.wait("v", st_free[sl])
                nc.vector.tensor_scalar(out=qk_stage[sl][:, so * 512:(so + 1) * 512],
                                        in0=ps_p[r][:],
                                        scalar1=bqk[:, oc:oc + 1], scalar2=None,
                                        op0=ALU.add).then_inc(ctx.sems["V"], 1)
                ev_st = ctx.inc("V", 1)
                pp_free[r] = ev_st
                if so == 3:
                    ctx.wait("y", ev_st)
                    dst = W["qt_s"] if oc < 8 else W["kt_s"]
                    base = (oc - 3) % 8
                    nc.sync.dma_start(
                        out=dst.rearrange("(c p) q -> p c q", p=128)[:, base:base + 4, tbs],
                        in_=qk_stage[sl][:].rearrange("p (c q) -> p c q", q=512)
                        ).then_inc(ctx.sems["DY"], 16)
                    st_free[sl] = ctx.inc("DY", 16)

        for i in range(NT):
            vs = i % 2
            ctx.wait("t", pa["col_done"][i])
            ctx.wait("t", ev_bv)
            ctx.wait("t", ev_ones)
            ctx.wait("t", pv_free)
            for half in range(2):
                for kc in range(NCH):
                    nc.tensor.matmul(ps_v[:, half * 512:(half + 1) * 512],
                                     xn_dim[kc][:, i * 128:(i + 1) * 128],
                                     wqkv[kc][:, 2048 + half * 512:2048 + (half + 1) * 512],
                                     start=(kc == 0), stop=False, skip_group_check=True)
                mm = nc.tensor.matmul(ps_v[:, half * 512:(half + 1) * 512], ones_r[:],
                                      bvr[:, half * 512:(half + 1) * 512],
                                      start=False, stop=True, skip_group_check=True)
            mm.then_inc(ctx.sems["T"], 1)
            ev_vp = ctx.inc("T", 1)
            ctx.wait("v", ev_vp)
            ctx.wait("v", vstg_free[vs])
            for half in range(2):
                nc.vector.tensor_copy(
                    v_stage[vs][:].rearrange("p (h u) -> p h u", u=66)[:, 8 * half:8 * half + 8, 0:64],
                    ps_v[:, half * 512:(half + 1) * 512].rearrange("p (h u) -> p h u", u=64))
            nc.vector.memset(v_stage[vs][:].rearrange("p (h u) -> p h u", u=66)[:, :, 64:65], 1.0
                             ).then_inc(ctx.sems["V"], 1)
            ev_vst = ctx.inc("V", 1)
            pv_free = ev_vst
            ctx.wait("y", ev_vst)
            nc.sync.dma_start(out=W["v_s"][i, :, :], in_=v_stage[vs][:]).then_inc(ctx.sems["DY"], 16)
            vstg_free[vs] = ctx.inc("DY", 16)
        ctx.barrier()

    with contextlib.ExitStack() as sf:
        kt_sb = [sf.enter_context(nc.sbuf_tensor(f"o_kt{c}", [128, 4096], BF16)) for c in range(NCH)]
        v_sb = [sf.enter_context(nc.sbuf_tensor(f"o_v{j}", [128, 1056], BF16)) for j in range(NT)]
        qt_sb = sf.enter_context(nc.sbuf_tensor("o_qt", [128, 4096], BF16))
        masks2 = sf.enter_context(nc.sbuf_tensor("o_mk", [128, 4 * 512], BF16))
        e_sb = [sf.enter_context(nc.sbuf_tensor(f"o_e{h}", [128, 1024], BF16)) for h in range(2)]
        zdiv = [sf.enter_context(nc.sbuf_tensor(f"o_z{h}", [64, 512], BF16)) for h in range(NH)]
        woext = sf.enter_context(nc.sbuf_tensor("o_wo", [64, NH * 1024], BF16))
        bo = sf.enter_context(nc.sbuf_tensor("o_bo", [1, 1024], BF16))
        ones_r2 = sf.enter_context(nc.sbuf_tensor("o_1r2", [1, 128], BF16))
        ident_bf2 = sf.enter_context(nc.sbuf_tensor("o_id2", [128, 128], BF16))
        rden = sf.enter_context(nc.sbuf_tensor("o_rd", [1, 1024], BF16))
        rb_sb = sf.enter_context(nc.sbuf_tensor("o_rb", [128, 512], BF16))

        def gdma2(dst, src):
            nc.gpsimd.dma_start(out=dst, in_=src).then_inc(ctx.sems["DG"], 16)
            return ctx.inc("DG", 16)

        for c in range(NCH):
            ev_kt = gdma2(kt_sb[c][:], W["kt_s"][c * 128:(c + 1) * 128, :])
        for j in range(NT):
            ev_v = gdma2(v_sb[j][:], W["v_s"][j, :, :])
        for m in range(4):
            ev_mk = gdma2(masks2[:, 512 * m:512 * m + 512], W["maskb"][m, :, :])
        ev_id2 = gdma2(ident_bf2[:], W["ident"][:, :])
        ev_wo = gdma2(woext[:], W["woext"][:, :])
        ev_bo = gdma2(bo[:], W["borow"][:, :])
        nc.vector.memset(ones_r2[:], 1.0).then_inc(ctx.sems["V"], 1)
        ev_ones2 = ctx.inc("V", 1)
        if "dbg_qts" in W:
            ctx.wait("g", ev_v)
            nc.gpsimd.dma_start(out=W["dbg_qts"][:, :], in_=kt_sb[0][:]
                                ).then_inc(ctx.sems["DG"], 16)
            nc.gpsimd.dma_start(out=W["dbg_vs"][:, :], in_=v_sb[0][:]
                                ).then_inc(ctx.sems["DG"], 16)
            ctx.inc("DG", 32)

        ps_s = [sf.enter_context(nc.psum_tensor(f"o_ps{j}", [128, 1024], F32))
                for j in range(2)]
        ps_z = sf.enter_context(nc.psum_tensor("o_pz", [128, 1024], F32))
        ps_y = sf.enter_context(nc.psum_tensor("o_py", [128, 1024], F32))
        cbufs = mk_phase_c_bufs(ctx, sf)

        qt_load = None
        qt_free = None
        s_free = [None, None]
        e_free = [None, None]
        z_free = {id(ps_z): [None, None], id(ps_y): [None, None]}
        ps_y_free = None
        ev_done = None
        pending = [None, None]
        zdiv_done = [None] * NH

        qt_src = W["qt_s"].rearrange("(c p) q -> p c q", p=128)
        for qb in range(NTB):
            ctx.wait("y", qt_free)
            nc.sync.dma_start(out=qt_sb[:].rearrange("p (c q) -> p c q", q=512),
                              in_=qt_src[:, :, qb * 512:(qb + 1) * 512]
                              ).then_inc(ctx.sems["DY"], 16)
            qt_load = ctx.inc("DY", 16)
            np_t = 4 * qb + 4
            for hp in range(8):
                h0 = 2 * hp
                z_t = ps_z if hp % 2 == 0 else ps_y

                def emit_scores(j):
                    jr = j % 2
                    m = j - 4 * qb
                    diag = m >= 0
                    for p in range(2):
                        ctx.wait("t", qt_load)
                        ctx.wait("t", ev_kt)
                        ctx.wait("t", ev_mk)
                        ctx.wait("t", ev_id2)
                        ctx.wait("t", s_free[jr])
                        mm = nc.tensor.matmul(
                            ps_s[jr][:, 512 * p:512 * p + 512],
                            kt_sb[hp][64 * p:64 * p + 64, j * 128:(j + 1) * 128],
                            qt_sb[64 * p:64 * p + 64, hp * 512:(hp + 1) * 512],
                            start=True, stop=not diag, skip_group_check=True,
                            tile_position=(64 * p, 0))
                    if diag:
                        for p in range(2):
                            mm = nc.tensor.matmul(
                                ps_s[jr][:, 512 * p:512 * p + 512],
                                ident_bf2[:], masks2[:, 512 * m:512 * m + 512],
                                start=False, stop=True, skip_group_check=True)
                    mm.then_inc(ctx.sems["T"], 1)
                    return ctx.inc("T", 1)

                def emit_exp(j, ev_sc):
                    jr = j % 2
                    ctx.wait("s", ev_sc)
                    ctx.wait("s", e_free[jr])
                    nc.scalar.activation(out=e_sb[jr][:], in_=ps_s[jr][:],
                                         func=AF.Exp, scale=0.125
                                         ).then_inc(ctx.sems["S"], 1)
                    ev_e = ctx.inc("S", 1)
                    s_free[jr] = ev_e
                    return ev_e

                def emit_av(j, ev_e):
                    jr = j % 2
                    last_evs = []
                    for p in range(2):
                        h = h0 + p
                        ctx.wait("t", ev_e)
                        ctx.wait("t", ev_v)
                        if j == 0:
                            ctx.wait("t", z_free[id(z_t)][p])
                            if z_t is ps_y:
                                ctx.wait("t", ps_y_free)
                        nc.tensor.matmul(
                            z_t[0:65, 512 * p:512 * p + 512],
                            v_sb[j][:, 66 * h:66 * h + 65],
                            e_sb[jr][:, 512 * p:512 * p + 512],
                            start=(j == 0), stop=(j == np_t - 1),
                            skip_group_check=True).then_inc(ctx.sems["T"], 1)
                        last_evs.append(ctx.inc("T", 1))
                    e_free[jr] = last_evs[-1]
                    return last_evs

                ev_sc_l = [None] * (np_t + 1)
                ev_e_l = [None] * np_t
                ev_sc_l[0] = emit_scores(0)
                if pending[0] is not None:
                    flush_divide(pending, ctx, nc, ps_s, rden, rb_sb, zdiv, z_free,
                                 s_free, zdiv_done, ones_r2, ev_ones2)
                ev_av = [None, None]
                for j in range(np_t):
                    if j + 1 < np_t:
                        ev_sc_l[j + 1] = emit_scores(j + 1)
                    ev_e_l[j] = emit_exp(j, ev_sc_l[j])
                    if j >= 1:
                        ev_av = emit_av(j - 1, ev_e_l[j - 1])
                ev_av = emit_av(np_t - 1, ev_e_l[np_t - 1])

                ctx.wait("s", ev_av[0])
                ctx.wait("s", ev_av[1])
                ctx.wait("s", pending[1])
                nc.scalar.activation(out=rden[:], in_=z_t[64:65, :], func=AF.Ln)
                nc.scalar.drain()
                nc.scalar.activation(out=rden[:], in_=rden[:], func=AF.Exp,
                                     scale=-1.0).then_inc(ctx.sems["S"], 1)
                ev_rd = ctx.inc("S", 1)
                pending[0] = (h0, z_t, ev_rd)
                if PARANOID:
                    ctx.barrier()
            if pending[0] is not None:
                flush_divide(pending, ctx, nc, ps_s, rden, rb_sb, zdiv, z_free,
                             s_free, zdiv_done, ones_r2, ev_ones2)
            if "dbg_zd" in W and qb == 0:
                ctx.barrier()
                for h in range(NH):
                    nc.gpsimd.dma_start(out=W["dbg_zd"][h, :, :], in_=zdiv[h][:]
                                        ).then_inc(ctx.sems["DG"], 16)
                    ctx.inc("DG", 16)
                ctx.wait("g", ("DG", ctx.cnt["DG"]))
                ctx.barrier()
            for u in range(4):
                i = 4 * qb + u
                ctx.wait("t", zdiv_done[NH - 1])
                ctx.wait("t", ev_wo)
                ctx.wait("t", ev_bo)
                ctx.wait("t", ps_y_free)
                for half in range(2):
                    for h in range(NH):
                        nc.tensor.matmul(ps_y[:, half * 512:(half + 1) * 512],
                                         zdiv[h][:, u * 128:(u + 1) * 128],
                                         woext[:, h * 1024 + half * 512:h * 1024 + (half + 1) * 512],
                                         start=(h == 0), stop=False, skip_group_check=True)
                    mm = nc.tensor.matmul(ps_y[:, half * 512:(half + 1) * 512], ones_r2[:],
                                          bo[:, half * 512:(half + 1) * 512],
                                          start=False, stop=True, skip_group_check=True)
                mm.then_inc(ctx.sems["T"], 1)
                ev_yo = ctx.inc("T", 1)
                if "dbg_y" in W and i == 0:
                    ctx.wait("v", ev_yo)
                    nc.vector.tensor_copy(e_sb[0][:], ps_y[:]).then_inc(ctx.sems["V"], 1)
                    dbg_ev = ctx.inc("V", 1)
                    ctx.wait("g", dbg_ev)
                    nc.gpsimd.dma_start(out=W["dbg_y"][:, :], in_=e_sb[0][:]).then_inc(ctx.sems["DG"], 16)
                    dbg_ev2 = ctx.inc("DG", 16)
                    ctx.wait("s", dbg_ev2)
                ev_done, ps_y_free = emit_phase_c(ctx, sf, io["x_in"], io["x_out"],
                                                  ps_y, ev_yo, i, cbufs)
            qt_free = ("T", ctx.cnt["T"])
    return ev_done


def l0a_weight_maps(inputs):
    W_Q = np.asarray(inputs["L0_attn_W_Q"])
    W_K = np.asarray(inputs["L0_attn_W_K"])
    W_V = np.asarray(inputs["L0_attn_W_V"])
    W_O = np.asarray(inputs["L0_attn_W_O"])
    b_Q = np.asarray(inputs["L0_attn_b_Q"]).reshape(-1)
    b_K = np.asarray(inputs["L0_attn_b_K"]).reshape(-1)
    b_V = np.asarray(inputs["L0_attn_b_V"]).reshape(-1)
    b_O = np.asarray(inputs["L0_attn_b_O"])
    cat = lambda w: w.transpose(1, 0, 2).reshape(D, NCH * 128)
    wqkv = np.concatenate([cat(W_Q), cat(W_K), cat(W_V)], axis=1)
    bqk = np.stack([b_Q, b_K]).reshape(2, D, 1)
    woext = np.zeros((64, 16 * 1024), np.float32)
    for h in range(16):
        woext[:, h * 1024:(h + 1) * 1024] = W_O[h]
    return {
        "w_qkv": np.ascontiguousarray(wqkv, np.float32),
        "w_bqk": np.ascontiguousarray(bqk, np.float32),
        "w_bvrow": np.ascontiguousarray(b_V.reshape(1, -1), np.float32),
        "w_oext": woext,
        "w_borow": np.ascontiguousarray(b_O.reshape(1, -1), np.float32),
    }


def emit_l1_attn(ctx, st, io, W):
    nc = ctx.nc
    HN = 4
    xn_dim = [st.enter_context(nc.sbuf_tensor(f"a_xnd{c}", [128, 4096], BF16)) for c in range(NCH)]
    ident_bf = st.enter_context(nc.sbuf_tensor("a_id", [128, 128], BF16))
    wqkv = [st.enter_context(nc.sbuf_tensor(f"a_wq{c}", [128, 768], BF16)) for c in range(NCH)]
    woT = [st.enter_context(nc.sbuf_tensor(f"a_wo{j}", [128, 1024], BF16)) for j in range(2)]
    mask0 = st.enter_context(nc.sbuf_tensor("a_m0", [128, 128], BF16))
    onesr = st.enter_context(nc.sbuf_tensor("a_ones", [1, 128], F32))
    onesc = st.enter_context(nc.sbuf_tensor("a_onesc", [128, 1], BF16))
    k_tok = [st.enter_context(nc.sbuf_tensor(f"a_kk{i}", [128, 256], BF16)) for i in range(NT)]
    q_scr = [st.enter_context(nc.sbuf_tensor(f"a_qs{i}", [128, 256], BF16)) for i in range(2)]
    v_tok = [st.enter_context(nc.sbuf_tensor(f"a_v{i}", [128, 264], BF16)) for i in range(NT)]
    qT = [st.enter_context(nc.sbuf_tensor(f"a_qT{j}", [128, 4096], BF16)) for j in range(2)]
    kT = [st.enter_context(nc.sbuf_tensor(f"a_kT{j}", [128, 4096], BF16)) for j in range(2)]
    yT = [st.enter_context(nc.sbuf_tensor(f"a_yT{j}", [128, 4096], BF16)) for j in range(2)]
    kvk = st.enter_context(nc.sbuf_tensor("a_kvk", [128, 132], BF16))
    at_sb = [st.enter_context(nc.sbuf_tensor(f"a_at{h}", [128, 128], BF16)) for h in range(HN)]
    rden = [st.enter_context(nc.sbuf_tensor(f"a_rden{i}", [1, 256], F32)) for i in range(2)]
    mtmp = [st.enter_context(nc.sbuf_tensor(f"a_mt{i}", [128, 512], F32)) for i in range(2)]
    psb = [st.enter_context(nc.sbuf_tensor(f"a_pp{i}", [128, 512], F32)) for i in range(2)]
    rn8 = st.enter_context(nc.sbuf_tensor("a_rn", [128, 8 * 2], F32))

    def gdma(dst, src):
        nc.gpsimd.dma_start(out=dst, in_=src).then_inc(ctx.sems["DG"], 16)
        return ctx.inc("DG", 16)

    ev_id = gdma(ident_bf[:], W["ident"][:, :])
    for c in range(NCH):
        ev_w = gdma(wqkv[c][:], W["wqkvT1"][c * 128:(c + 1) * 128, :])
    for j in range(2):
        ev_w = gdma(woT[j][:], W["woT"][j * 128:(j + 1) * 128, :])
    ev_mk0 = gdma(mask0[:], W["masks"][0, :, 0:128])
    nc.vector.memset(onesr[:], 1.0)
    nc.vector.memset(onesc[:], 1.0).then_inc(ctx.sems["V"], 1)
    ev_ones = ctx.inc("V", 1)

    pa = emit_phase_a(ctx, st, io["x_in"], xn_dim, ev_id, ident_bf)
    ctx.barrier()
    if "dbg_xnd" in W:
        nc.gpsimd.dma_start(out=W["dbg_xnd"][:, :], in_=xn_dim[0][:]).then_inc(ctx.sems["DG"], 16)
        ctx.inc("DG", 16)

    with contextlib.ExitStack() as ps1:
        ps_qkv = [ps1.enter_context(nc.psum_tensor(f"a_pq{i}", [128, 1024], F32)) for i in range(2)]
        pstr = [ps1.enter_context(nc.psum_tensor(f"a_ptr{i}", [128, 128], BF16)) for i in range(2)]
        qkv_free = [None, None]
        scr_free = [None, None]
        tr_free = [None, None]
        trq_free = [None, None]
        ev_qkv_l = [None] * NT

        def emit_qkv(i):
            r = i % 2
            ctx.wait("t", ev_w)
            ctx.wait("t", qkv_free[r])
            for seg, (lo, hi) in enumerate(((0, 512), (512, 768))):
                for kc in range(NCH):
                    mm = nc.tensor.matmul(ps_qkv[r][:, lo:hi],
                                          xn_dim[kc][:, i * 128:(i + 1) * 128],
                                          wqkv[kc][:, lo:hi], start=(kc == 0),
                                          stop=(kc == NCH - 1), skip_group_check=True)
            mm.then_inc(ctx.sems["T"], 1)
            return ctx.inc("T", 1)

        def emit_post(i):
            r = i % 2
            ev_qkv = ev_qkv_l[i]
            ctx.wait("v", ev_qkv)
            ctx.wait("v", scr_free[r])
            nc.vector.tensor_scalar(out=mtmp[r][:], in0=ps_qkv[r][:, 0:512],
                                    scalar1=0.0, scalar2=None, op0=ALU.min
                                    ).then_inc(ctx.sems["V"], 1)
            ev_min = ctx.inc("V", 1)
            ctx.wait("s", ev_min)
            nc.scalar.activation(out=mtmp[r][:], in_=mtmp[r][:], func=AF.Exp
                                 ).then_inc(ctx.sems["S"], 1)
            ev_exp = ctx.inc("S", 1)
            ctx.wait("v", ev_exp)
            nc.vector.scalar_tensor_tensor(out=psb[r][:], in0=ps_qkv[r][:, 0:512],
                                           scalar=0.0, in1=mtmp[r][:],
                                           op0=ALU.max, op1=ALU.add)
            nc.vector.drain()
            nc.vector.tensor_tensor(out=mtmp[r][:], in0=psb[r][:], in1=psb[r][:],
                                    op=ALU.mult)
            nc.vector.drain()
            nc.vector.tensor_reduce(rn8[:, 8 * r:8 * r + 8],
                                    mtmp[r][:].rearrange("p (h d) -> p h d", d=64),
                                    axis=AX.X, op=ALU.add).then_inc(ctx.sems["V"], 1)
            ev_s8 = ctx.inc("V", 1)
            ctx.wait("s", ev_s8)
            nc.scalar.activation(out=rn8[:, 8 * r:8 * r + 8], in_=rn8[:, 8 * r:8 * r + 8],
                                 func=AF.Ln)
            nc.scalar.drain()
            nc.scalar.activation(out=rn8[:, 8 * r:8 * r + 8], in_=rn8[:, 8 * r:8 * r + 8],
                                 func=AF.Exp, scale=0.5).then_inc(ctx.sems["S"], 1)
            ev_sqrt = ctx.inc("S", 1)
            ctx.wait("v", ev_sqrt)
            nc.vector.tensor_scalar(out=rn8[:, 8 * r:8 * r + 8], in0=rn8[:, 8 * r:8 * r + 8],
                                    scalar1=1e-6, scalar2=None, op0=ALU.add)
            nc.vector.drain()
            nc.vector.reciprocal(out=rn8[:, 8 * r:8 * r + 8], in_=rn8[:, 8 * r:8 * r + 8]
                                 ).then_inc(ctx.sems["V"], 1)
            ev_rn = ctx.inc("V", 1)
            ctx.wait("g", ev_rn)
            ctx.wait("g", trq_free[r])
            for hh in range(8):
                dstt = q_scr[r] if hh < 4 else k_tok[i]
                dco = 64 * (hh % 4)
                gg = nc.gpsimd.tensor_scalar(out=dstt[:, dco:dco + 64],
                                             in0=psb[r][:, 64 * hh:64 * hh + 64],
                                             scalar1=rn8[:, 8 * r + hh:8 * r + hh + 1],
                                             scalar2=None, op0=ALU.mult)
            gg.then_inc(ctx.sems["G"], 1)
            ev_qk = ctx.inc("G", 1)
            for h in range(4):
                nc.vector.tensor_copy(v_tok[i][:, 66 * h:66 * h + 64],
                                      ps_qkv[r][:, 512 + 64 * h:512 + 64 * h + 64])
            nc.vector.memset(v_tok[i][:].rearrange("p (h u) -> p h u", u=66)[:, :, 64:65], 1.0
                             ).then_inc(ctx.sems["V"], 1)
            ev_tok = ctx.inc("V", 1)
            qkv_free[r] = ev_tok
            scr_free[r] = [ev_tok, ev_qk]
            for c4 in range(4):
                pb = c4 % 2
                srct = q_scr[r] if c4 < 2 else k_tok[i]
                sco = 128 * (c4 % 2)
                ctx.wait("t", ev_qk)
                ctx.wait("t", tr_free[pb])
                nc.tensor.transpose(pstr[pb][:], srct[:, sco:sco + 128],
                                    ident_bf[:]).then_inc(ctx.sems["T"], 1)
                ev_tr = ctx.inc("T", 1)
                if c4 == 1:
                    trq_free[r] = ev_tr
                dst = (qT if c4 < 2 else kT)[c4 % 2]
                ctx.wait("v", ev_tr)
                nc.vector.tensor_copy(dst[:, i * 128:(i + 1) * 128], pstr[pb][:]
                                      ).then_inc(ctx.sems["V"], 1)
                tr_free[pb] = ctx.inc("V", 1)

        for i in range(NT):
            ev_qkv_l[i] = emit_qkv(i)
            if i >= 1:
                emit_post(i - 1)
        emit_post(NT - 1)
        ctx.barrier()
    if "dbg_qt" in W:
        nc.gpsimd.dma_start(out=W["dbg_qt"][:, :], in_=qT[0][:, 0:256]).then_inc(ctx.sems["DG"], 16)
        nc.gpsimd.dma_start(out=W["dbg_kt"][:, :], in_=kT[0][:, 0:256]).then_inc(ctx.sems["DG"], 16)
        nc.gpsimd.dma_start(out=W["dbg_vt"][:, :], in_=v_tok[0][:]).then_inc(ctx.sems["DG"], 16)
        nc.gpsimd.dma_start(out=W["dbg_ktok"][:, :], in_=k_tok[0][:]).then_inc(ctx.sems["DG"], 16)
        ctx.inc("DG", 64)

    with contextlib.ExitStack() as ps2:
        ps_kv = ps2.enter_context(nc.psum_tensor("a_pkv", [128, 132], F32))
        ps_a = [ps2.enter_context(nc.psum_tensor(f"a_pa{i}", [128, 384], F32)) for i in range(2)]
        ps_y = [ps2.enter_context(nc.psum_tensor(f"a_py{i}", [128, 128], F32)) for i in range(2)]
        ps_bc = ps2.enter_context(nc.psum_tensor("a_pbc", [128, 512], F32))

        nc.vector.memset(kvk[:], 0.0).then_inc(ctx.sems["V"], 1)
        ev_kvk_evac = ctx.inc("V", 1)
        mask_done = None
        y_free = None
        d_free = None
        bc_free = None

        ev_d_prev = None
        ev_bc_prev = None
        for c in range(NT):
            cs = slice(c * 128, (c + 1) * 128)
            ev_at = None
            for pair in range(2):
                ctx.wait("t", mask_done)
                for p in range(2):
                    mm = nc.tensor.matmul(ps_a[p][:, 0:128], kT[pair][64 * p:64 * p + 64, cs],
                                          qT[pair][64 * p:64 * p + 64, cs],
                                          start=True, stop=True, skip_group_check=True,
                                          tile_position=(64 * p, 0))
                mm.then_inc(ctx.sems["T"], 1)
                ev_sc = ctx.inc("T", 1)
                ctx.wait("v", ev_sc)
                ctx.wait("v", ev_d_prev)
                ctx.wait("v", ev_mk0)
                for p in range(2):
                    h = 2 * pair + p
                    last = nc.vector.tensor_tensor(out=at_sb[h][:], in0=ps_a[p][:, 0:128],
                                                   in1=mask0[:], op=ALU.mult)
                last.then_inc(ctx.sems["V"], 1)
                mask_done = ctx.inc("V", 1)
            ev_at = mask_done

            ctx.wait("t", ev_at)
            ctx.wait("t", y_free)
            for h in range(HN):
                bank = h // 2
                b = 64 * (h % 2)
                nc.tensor.matmul(ps_y[bank][b:b + 64, 0:128],
                                 v_tok[c][:, 66 * h:66 * h + 64], at_sb[h][:],
                                 start=True, stop=False, skip_group_check=True,
                                 tile_position=(0, b))
            ctx.wait("t", ev_kvk_evac)
            for h in range(HN):
                bank = h // 2
                b = 64 * (h % 2)
                j = h // 2
                mm = nc.tensor.matmul(ps_y[bank][b:b + 64, 0:128],
                                      kvk[b:b + 64, 66 * j:66 * j + 64],
                                      qT[j][b:b + 64, cs],
                                      start=False, stop=True, skip_group_check=True,
                                      tile_position=(b, b))
            mm.then_inc(ctx.sems["T"], 1)
            ev_y = ctx.inc("T", 1)

            ctx.wait("t", d_free)
            for h in range(HN):
                dk = h % 2
                dco = 128 + 128 * (h // 2)
                nc.tensor.matmul(ps_a[dk][0:1, dco:dco + 128], onesc[:], at_sb[h][:],
                                 start=False, stop=False, skip_group_check=True)
            for h in range(HN):
                b = 64 * (h % 2)
                j = h // 2
                dk = h % 2
                dco = 128 + 128 * (h // 2)
                mm = nc.tensor.matmul(ps_a[dk][0:1, dco:dco + 128],
                                      kvk[b:b + 64, 66 * j + 64:66 * j + 65],
                                      qT[j][b:b + 64, cs],
                                      start=False, stop=True, skip_group_check=True,
                                      tile_position=(b, 0))
            mm.then_inc(ctx.sems["T"], 1)
            ev_d = ctx.inc("T", 1)

            for h in range(HN):
                b = 64 * (h % 2)
                j = h // 2
                mm = nc.tensor.matmul(ps_kv[b:b + 64, 66 * j:66 * j + 65],
                                      k_tok[c][:, 64 * h:64 * h + 64],
                                      v_tok[c][:, 66 * h:66 * h + 65],
                                      start=(c == 0 and h < 2), stop=(c == NT - 1),
                                      skip_group_check=True, tile_position=(0, b))
            mm.then_inc(ctx.sems["T"], 1)
            ev_upd = ctx.inc("T", 1)
            ctx.wait("v", ev_upd)
            nc.vector.tensor_copy(kvk[:], ps_kv[:]).then_inc(ctx.sems["V"], 1)
            ev_kvk_evac = ctx.inc("V", 1)

            ctx.wait("v", ev_y)
            for j in range(2):
                nc.vector.tensor_copy(yT[j][:, cs], ps_y[j][:])
            nc.vector.drain()
            nc.vector.sem_inc(ctx.sems["V"], 1)
            y_free = ctx.inc("V", 1)
            ctx.wait("v", ev_d)
            ctx.wait("v", ev_bc_prev)
            for dk in range(2):
                nc.vector.tensor_scalar(out=rden[dk][:], in0=ps_a[dk][0:1, 128:384],
                                        scalar1=1e-6, scalar2=None, op0=ALU.add)
            nc.vector.drain()
            for dk in range(2):
                last = nc.vector.reciprocal(out=rden[dk][:], in_=rden[dk][:])
            last.then_inc(ctx.sems["V"], 1)
            ev_rden = ctx.inc("V", 1)
            d_free = ev_rden

            ctx.wait("t", ev_rden)
            ctx.wait("t", ev_ones)
            ctx.wait("t", bc_free)
            for h in range(HN):
                dk = h % 2
                dco = 128 * (h // 2)
                mm = nc.tensor.matmul(ps_bc[:, 128 * h:128 * h + 128], onesr[:],
                                      rden[dk][0:1, dco:dco + 128],
                                      start=(h == 0), stop=(h == HN - 1),
                                      skip_group_check=True)
            mm.then_inc(ctx.sems["T"], 1)
            ev_bc = ctx.inc("T", 1)
            ctx.wait("v", ev_bc)
            ctx.wait("v", y_free)
            for h in range(HN):
                b = 64 * (h % 2)
                j = h // 2
                last = nc.vector.tensor_tensor(out=yT[j][b:b + 64, cs],
                                               in0=yT[j][b:b + 64, cs],
                                               in1=ps_bc[b:b + 64, 128 * h:128 * h + 128],
                                               op=ALU.mult)
            last.then_inc(ctx.sems["V"], 1)
            div_done = ctx.inc("V", 1)
            bc_free = div_done
            ev_d_prev = ev_d
            ev_bc_prev = ev_bc
        ctx.barrier()
    if "dbg_y0" in W:
        nc.gpsimd.dma_start(out=W["dbg_y0"][:, :], in_=yT[0][:]).then_inc(ctx.sems["DG"], 16)
        nc.gpsimd.dma_start(out=W["dbg_y1"][:, :], in_=yT[1][:]).then_inc(ctx.sems["DG"], 16)
        nc.gpsimd.dma_start(out=W["dbg_kvk"][:, :], in_=kvk[:]).then_inc(ctx.sems["DG"], 16)
        ctx.inc("DG", 48)

    with contextlib.ExitStack() as ps3:
        ps_o = [ps3.enter_context(nc.psum_tensor(f"a_po{r}", [128, 1024], F32)) for r in range(2)]
        cbufs = mk_phase_c_bufs(ctx, ps3)
        ps_o_free = [None, None]
        ev_done = None
        for i in range(NT):
            yr = i % 2
            ctx.wait("t", ps_o_free[yr])
            for half in range(2):
                for j in range(2):
                    mm = nc.tensor.matmul(ps_o[yr][:, half * 512:(half + 1) * 512],
                                          yT[j][:, i * 128:(i + 1) * 128],
                                          woT[j][:, half * 512:(half + 1) * 512],
                                          start=(j == 0), stop=(j == 1),
                                          skip_group_check=True)
            mm.then_inc(ctx.sems["T"], 1)
            ev_yo = ctx.inc("T", 1)
            ev_done, yfree = emit_phase_c(ctx, st, io["x_in"], io["x_out"],
                                          ps_o[yr], ev_yo, i, cbufs)
            ps_o_free[yr] = yfree
    return ev_done


def l1a_weight_maps(inputs):
    Wq = np.asarray(inputs["L1_attn_Wq"])
    Wk = np.asarray(inputs["L1_attn_Wk"])
    Wv = np.asarray(inputs["L1_attn_Wv"])
    Wo = np.asarray(inputs["L1_attn_Wo"])
    wqkvT1 = np.concatenate([Wq.T, Wk.T, Wv.T], axis=1)
    return {
        "w_qkvT1": np.ascontiguousarray(wqkvT1, np.float32),
        "w_oT": np.ascontiguousarray(Wo.T, np.float32),
    }


def make_masks():
    m = np.zeros((4, 128, 512), np.float32)
    for j in range(4):
        off = 128 * j
        p = np.arange(128)[:, None]
        q = np.arange(512)[None, :]
        m[j] = (p + off <= q).astype(np.float32)
    return m


def make_maskb():
    return -1e6 * (1.0 - make_masks())


BLOCK_EMITTERS = {}


def emit_l1_ffn(ctx, st, io, W):
    nc = ctx.nc
    NE = 8
    xn_dim = [st.enter_context(nc.sbuf_tensor(f"m_xnd{c}", [128, 4096], BF16)) for c in range(NCH)]
    xn_tok = [st.enter_context(nc.sbuf_tensor(f"m_xnt{i}", [128, 1024], BF16)) for i in range(NT)]
    ident_bf = st.enter_context(nc.sbuf_tensor("m_id", [128, 128], BF16))
    wbase = [st.enter_context(nc.sbuf_tensor(f"m_wb{c}", [128, 1024], BF16)) for c in range(NCH)]
    bbase_r = st.enter_context(nc.sbuf_tensor("m_bb", [1, 1024], BF16))
    cols = st.enter_context(nc.sbuf_tensor("m_cols", [128, 16], F32))
    sx_sb = st.enter_context(nc.sbuf_tensor("m_sx", [128, 8], F32))
    sq_scr = st.enter_context(nc.sbuf_tensor("m_sq", [128, 8], F32))
    onesbf_r = st.enter_context(nc.sbuf_tensor("m_1r", [1, 128], BF16))
    onesbf_c = st.enter_context(nc.sbuf_tensor("m_1c", [128, 1], BF16))
    onesf_c = st.enter_context(nc.sbuf_tensor("m_1fc", [128, 1], F32))
    onesf_r = st.enter_context(nc.sbuf_tensor("m_1fr", [1, 128], F32))
    kxones = st.enter_context(nc.sbuf_tensor("m_kx1", [128, 1], F32))
    rows8 = st.enter_context(nc.sbuf_tensor("m_r8", [1, 40], F32))
    iota8 = st.enter_context(nc.sbuf_tensor("m_io", [1, 8], F32))
    ohc = st.enter_context(nc.sbuf_tensor("m_ohc", [128, 8], F32))
    ohcol = st.enter_context(nc.sbuf_tensor("m_ohl", [8, 1], F32))
    wadn_sel = [st.enter_context(nc.sbuf_tensor(f"m_was{c}", [128, 64], BF16)) for c in range(NCH)]
    waup_sel = st.enter_context(nc.sbuf_tensor("m_wus", [64, 1024], BF16))
    basel = st.enter_context(nc.sbuf_tensor("m_bas", [64, 1], F32))
    bupsel = st.enter_context(nc.sbuf_tensor("m_bus", [1, 1024], BF16))

    def gdma(dst, src):
        nc.gpsimd.dma_start(out=dst, in_=src).then_inc(ctx.sems["DG"], 16)
        return ctx.inc("DG", 16)

    ev_id = gdma(ident_bf[:], W["ident"][:, :])
    for c in range(NCH):
        gdma(wbase[c][:], W["wbase"][c * 128:(c + 1) * 128, :])
    gdma(bbase_r[:], W["bbase"][:, :])
    ev_wres = gdma(iota8[:], W["iota8"][:, :])
    nc.vector.memset(onesbf_r[:], 1.0)
    nc.vector.memset(onesbf_c[:], 1.0)
    nc.vector.memset(onesf_c[:], 1.0)
    nc.vector.memset(onesf_r[:], 1.0)
    nc.vector.memset(kxones[:], 1024.0)
    nc.vector.memset(cols[:], 0.0).then_inc(ctx.sems["V"], 1)
    ev_ones = ctx.inc("V", 1)

    stats = st.enter_context(nc.sbuf_tensor("m_st", [128, 4 * NT], F32))
    with contextlib.ExitStack() as pa_sc:
        pa = emit_phase_a(ctx, pa_sc, io["x_in"], xn_dim, ev_id, ident_bf,
                          xn_tok=xn_tok, stats_sb=stats, ring=4)
        ctx.barrier()

    with contextlib.ExitStack() as sp:
        ps_sm = sp.enter_context(nc.psum_tensor("m_psm", [1, 24], F32))
        bd2t = [sp.enter_context(nc.sbuf_tensor(f"m_b2t{c}", [128, 8], F32)) for c in range(NCH)]
        for c in range(NCH):
            ev_wres = gdma(bd2t[c][:], W["bd2t"][c * 128:(c + 1) * 128, :])
        ctx.wait("v", ev_wres)
        ctx.wait("t", ev_wres)
        for c in range(NCH):
            nc.vector.tensor_reduce(sx_sb[:, c:c + 1], xn_dim[c][:], axis=AX.X, op=ALU.add)
            nc.vector.drain()
        for i in range(NT):
            nc.vector.tensor_scalar(out=cols[:, 1:2], in0=stats[:, 4 * i + 1:4 * i + 2],
                                    scalar1=stats[:, 4 * i + 2:4 * i + 3],
                                    scalar2=stats[:, 4 * i + 2:4 * i + 3],
                                    op0=ALU.mult, op1=ALU.mult)
            nc.vector.drain()
            nc.vector.tensor_tensor(out=cols[:, 0:1], in0=cols[:, 0:1], in1=cols[:, 1:2],
                                    op=ALU.add)
            nc.vector.drain()
        nc.vector.sem_inc(ctx.sems["V"], 1)
        ev_kx = ctx.inc("V", 1)
        ev_sq_mm = None
        for c in range(NCH):
            ctx.wait("v", ev_sq_mm)
            nc.vector.tensor_tensor(out=sq_scr[:], in0=bd2t[c][:], in1=bd2t[c][:], op=ALU.mult
                                    ).then_inc(ctx.sems["V"], 1)
            ev_sq = ctx.inc("V", 1)
            ctx.wait("t", ev_sq)
            nc.tensor.matmul(ps_sm[:, 8:16], onesf_c[:], sq_scr[:],
                             start=(c == 0), stop=(c == NCH - 1), skip_group_check=True
                             ).then_inc(ctx.sems["T"], 1)
            ev_sq_mm = ctx.inc("T", 1)
        ctx.wait("t", ev_kx)
        for c in range(NCH):
            nc.tensor.matmul(ps_sm[:, 0:8], sx_sb[:, c:c + 1], bd2t[c][:],
                             start=(c == 0), stop=(c == NCH - 1), skip_group_check=True)
        nc.tensor.matmul(ps_sm[:, 16:17], kxones[:], cols[:, 0:1],
                         start=True, stop=True, skip_group_check=True).then_inc(ctx.sems["T"], 1)
        ev_dots = ctx.inc("T", 1)
        ctx.wait("v", ev_dots)
        nc.vector.tensor_scalar(out=rows8[:, 8:16], in0=ps_sm[:, 8:16], scalar1=4096.0,
                                scalar2=None, op0=ALU.mult)
        nc.vector.tensor_scalar(out=rows8[:, 16:24], in0=ps_sm[:, 0:8], scalar1=-2.0,
                                scalar2=None, op0=ALU.mult)
        nc.vector.drain()
        nc.vector.tensor_tensor(out=rows8[:, 0:8], in0=rows8[:, 8:16], in1=rows8[:, 16:24],
                                op=ALU.add)
        nc.vector.drain()
        nc.vector.tensor_scalar(out=rows8[:, 0:8], in0=rows8[:, 0:8],
                                scalar1=ps_sm[0:1, 16:17], scalar2=None, op0=ALU.add
                                ).then_inc(ctx.sems["V"], 1)
        ev_crow = ctx.inc("V", 1)
        ctx.barrier()

    with contextlib.ExitStack() as se:
        wadn_all = [se.enter_context(nc.sbuf_tensor(f"m_wad{c}", [128, 512], BF16)) for c in range(NCH)]
        waup_all = se.enter_context(nc.sbuf_tensor("m_wau", [64, 8192], BF16))
        badn_all = se.enter_context(nc.sbuf_tensor("m_bad", [8, 64], F32))
        baup_all = se.enter_context(nc.sbuf_tensor("m_bau", [8, 1024], F32))
        wd1 = [se.enter_context(nc.sbuf_tensor(f"m_w1{c}", [128, 256], BF16)) for c in range(NCH)]
        wd2 = [se.enter_context(nc.sbuf_tensor(f"m_w2{j}", [128, 1024], F32)) for j in range(2)]
        bd1r = se.enter_context(nc.sbuf_tensor("m_b1r", [1, 256], BF16))
        bd2r = se.enter_context(nc.sbuf_tensor("m_b2r", [1, 1024], BF16))
        h_tok = [se.enter_context(nc.sbuf_tensor(f"m_h{i}", [128, 256], BF16)) for i in range(2)]
        H_sb = [se.enter_context(nc.sbuf_tensor(f"m_H{j}", [128, 256], F32)) for j in range(2)]
        nsh = se.enter_context(nc.sbuf_tensor("m_nsh", [1, 256], BF16))
        uprod = se.enter_context(nc.sbuf_tensor("m_up", [128, 256], F32))
        for c in range(NCH):
            gdma(wadn_all[c][:], W["wadn"][c * 128:(c + 1) * 128, :])
        gdma(waup_all[:], W["waup"][:, :])
        gdma(badn_all[:], W["badn"][:, :])
        ev_wres = gdma(baup_all[:], W["baup"][:, :])
        ps_h = se.enter_context(nc.psum_tensor("m_ph", [128, 256], F32))
        ps_H = [se.enter_context(nc.psum_tensor(f"m_pH{j}", [128, 256], F32)) for j in range(2)]
        ps_M = [se.enter_context(nc.psum_tensor(f"m_pM{j}", [128, 512], F32)) for j in range(4)]
        ps_L = se.enter_context(nc.psum_tensor("m_pL", [1, 512], F32))

        ev_wd_ready = None
        ps_h_free = None
        h_read = [None, None]
        ev_Hev = None
        ev_loss_mm = None
        for e in range(NE):
            ctx.wait("g", ev_wd_ready)
            for c in range(NCH):
                gdma(wd1[c][:], W["wd1"][e, c * 128:(c + 1) * 128, :])
            for j in range(2):
                gdma(wd2[j][:], W["wd2"][e, j * 128:(j + 1) * 128, :])
            gdma(bd1r[:], W["bd1"][e:e + 1, :])
            ev_wd = gdma(bd2r[:], W["bd2"][e:e + 1, :])

            ev_m = None
            for i in range(NT):
                sl = i % 2
                ctx.wait("t", ev_wd)
                ctx.wait("t", ev_ones)
                ctx.wait("t", ps_h_free)
                for kc in range(NCH):
                    nc.tensor.matmul(ps_h[:], xn_dim[kc][:, i * 128:(i + 1) * 128],
                                     wd1[kc][:], start=(kc == 0), stop=False,
                                     skip_group_check=True)
                nc.tensor.matmul(ps_h[:], onesbf_r[:], bd1r[:], start=False, stop=True,
                                 skip_group_check=True).then_inc(ctx.sems["T"], 1)
                ev_hp = ctx.inc("T", 1)
                ctx.wait("s", ev_hp)
                ctx.wait("s", h_read[sl])
                nc.scalar.activation(out=h_tok[sl][:], in_=ps_h[:], func=AF.Relu
                                     ).then_inc(ctx.sems["S"], 1)
                ev_hs = ctx.inc("S", 1)
                ps_h_free = ev_hs
                ctx.wait("t", ev_hs)
                for mh in range(2):
                    nc.tensor.matmul(ps_H[mh][:], h_tok[sl][:, mh * 128:(mh + 1) * 128],
                                     h_tok[sl][:], start=(i == 0), stop=(i == NT - 1),
                                     skip_group_check=True)
                for mh in range(2):
                    for nh2 in range(2):
                        nc.tensor.matmul(ps_M[2 * mh + nh2][:],
                                         h_tok[sl][:, mh * 128:(mh + 1) * 128],
                                         xn_tok[i][:, nh2 * 512:(nh2 + 1) * 512],
                                         start=(i == 0), stop=False, skip_group_check=True)
                nc.tensor.matmul(ps_L[:, 8:264], onesbf_c[:], h_tok[sl][:],
                                 start=(i == 0), stop=False,
                                 skip_group_check=True).then_inc(ctx.sems["T"], 1)
                ev_m = ctx.inc("T", 1)
                h_read[sl] = ev_m
            ctx.wait("v", ev_m)
            nc.vector.tensor_scalar(out=nsh[:], in0=ps_L[0:1, 8:264], scalar1=-1.0,
                                    scalar2=None, op0=ALU.mult)
            for mh in range(2):
                nc.vector.tensor_copy(H_sb[mh][:], ps_H[mh][:])
            nc.vector.drain()
            nc.vector.sem_inc(ctx.sems["V"], 1)
            ev_Hev = ctx.inc("V", 1)
            ctx.wait("t", ev_Hev)
            for mh in range(2):
                for nh2 in range(2):
                    mmu = nc.tensor.matmul(ps_M[2 * mh + nh2][:], nsh[:, mh * 128:(mh + 1) * 128],
                                           bd2r[:, nh2 * 512:(nh2 + 1) * 512], start=False,
                                           stop=True, skip_group_check=True)
            mmu.then_inc(ctx.sems["T"], 1)
            ev_U = ctx.inc("T", 1)
            ctx.wait("v", ev_U)
            for b8 in range(8):
                b4, pc = b8 // 2, b8 % 2
                mh, nh2 = b4 // 2, b4 % 2
                nc.vector.tensor_tensor(out=uprod[:], in0=ps_M[b4][:, pc * 256:(pc + 1) * 256],
                                        in1=wd2[mh][:, nh2 * 512 + pc * 256:nh2 * 512 + (pc + 1) * 256],
                                        op=ALU.mult)
                nc.vector.drain()
                nc.vector.tensor_reduce(cols[:, 2 + b8:3 + b8], uprod[:], axis=AX.X, op=ALU.add)
                nc.vector.drain()
            nc.vector.tensor_reduce(cols[:, 10:11], cols[:, 2:10], axis=AX.X, op=ALU.add)
            nc.vector.drain()
            nc.vector.tensor_scalar(out=cols[:, 10:11], in0=cols[:, 10:11], scalar1=-2.0,
                                    scalar2=None, op0=ALU.mult).then_inc(ctx.sems["V"], 1)
            ev_ucol = ctx.inc("V", 1)
            ctx.wait("t", ev_ucol)
            nc.tensor.matmul(ps_L[:, e:e + 1], onesf_c[:], cols[:, 10:11], start=False,
                             stop=False, skip_group_check=True)
            for mh in range(2):
                for nh2 in range(2):
                    for jc in range(2):
                        mmh = nc.tensor.matmul(ps_M[2 * mh + nh2][:],
                                               H_sb[jc][:, mh * 128:(mh + 1) * 128],
                                               wd2[jc][:, nh2 * 512:(nh2 + 1) * 512],
                                               start=(jc == 0), stop=(jc == 1),
                                               skip_group_check=True)
            mmh.then_inc(ctx.sems["T"], 1)
            ev_HW = ctx.inc("T", 1)
            ctx.wait("v", ev_HW)
            for b8 in range(8):
                b4, pc = b8 // 2, b8 % 2
                mh, nh2 = b4 // 2, b4 % 2
                nc.vector.tensor_tensor(out=uprod[:], in0=ps_M[b4][:, pc * 256:(pc + 1) * 256],
                                        in1=wd2[mh][:, nh2 * 512 + pc * 256:nh2 * 512 + (pc + 1) * 256],
                                        op=ALU.mult)
                nc.vector.drain()
                nc.vector.tensor_reduce(cols[:, 2 + b8:3 + b8], uprod[:], axis=AX.X, op=ALU.add)
                nc.vector.drain()
            nc.vector.tensor_reduce(cols[:, 10:11], cols[:, 2:10], axis=AX.X, op=ALU.add
                                    ).then_inc(ctx.sems["V"], 1)
            ev_acol = ctx.inc("V", 1)
            ctx.wait("t", ev_acol)
            nc.tensor.matmul(ps_L[:, e:e + 1], onesf_c[:], cols[:, 10:11], start=False,
                             stop=(e == NE - 1), skip_group_check=True).then_inc(ctx.sems["T"], 1)
            ev_loss_mm = ctx.inc("T", 1)
            ev_wd_ready = ev_acol
            ctx.wait("g", ev_acol)
            ctx.wait("g", ev_loss_mm)

        ctx.wait("v", ev_loss_mm)
        ctx.wait("v", ev_crow)
        nc.vector.tensor_tensor(out=rows8[:, 0:8], in0=rows8[:, 0:8], in1=ps_L[0:1, 0:8],
                                op=ALU.add)
        nc.vector.drain()
        nc.vector.tensor_reduce(rows8[:, 32:33], rows8[:, 0:8], axis=AX.X, op=ALU.min)
        nc.vector.drain()
        nc.vector.tensor_scalar(out=rows8[:, 8:16], in0=rows8[:, 0:8],
                                scalar1=rows8[:, 32:33], scalar2=None, op0=ALU.is_equal)
        nc.vector.drain()
        nc.vector.tensor_scalar(out=rows8[:, 8:16], in0=rows8[:, 8:16], scalar1=-1e9,
                                scalar2=1e9, op0=ALU.mult, op1=ALU.add)
        nc.vector.drain()
        ctx.wait("v", ev_wres)
        nc.vector.tensor_tensor(out=rows8[:, 8:16], in0=rows8[:, 8:16], in1=iota8[:], op=ALU.add)
        nc.vector.drain()
        nc.vector.tensor_reduce(rows8[:, 33:34], rows8[:, 8:16], axis=AX.X, op=ALU.min)
        nc.vector.drain()
        nc.vector.tensor_scalar(out=rows8[:, 16:24], in0=iota8[:], scalar1=rows8[:, 33:34],
                                scalar2=None, op0=ALU.is_equal).then_inc(ctx.sems["V"], 1)
        ev_oh = ctx.inc("V", 1)
        ctx.wait("t", ev_oh)
        ctx.wait("t", ev_Hev)
        nc.tensor.matmul(ps_H[0][:, 0:8], onesf_r[:], rows8[:, 16:24],
                         start=True, stop=True, skip_group_check=True)
        nc.tensor.transpose(ps_H[1][0:8, 0:1], rows8[:, 16:24], onesf_c[0:1, 0:1]
                            ).then_inc(ctx.sems["T"], 1)
        ev_bc = ctx.inc("T", 1)
        ctx.wait("v", ev_bc)
        nc.vector.tensor_copy(ohc[:], ps_H[0][:, 0:8])
        nc.vector.tensor_copy(ohcol[:], ps_H[1][0:8, 0:1])
        nc.vector.drain()
        nc.vector.sem_inc(ctx.sems["V"], 1)
        ev_ohc = ctx.inc("V", 1)
        ctx.wait("t", ev_ohc)
        nc.tensor.matmul(ps_H[0][0:64, 16:17], badn_all[:], ohcol[:],
                         start=True, stop=True, skip_group_check=True)
        mmb = None
        for half in range(2):
            mmb = nc.tensor.matmul(ps_M[0][0:1, 0:512] if half == 0 else ps_M[1][0:1, 0:512],
                                   ohcol[:], baup_all[:, half * 512:(half + 1) * 512],
                                   start=True, stop=True, skip_group_check=True)
        mmb.then_inc(ctx.sems["T"], 1)
        ev_bsel = ctx.inc("T", 1)
        ctx.wait("v", ev_bsel)
        nc.vector.tensor_copy(basel[:], ps_H[0][0:64, 16:17])
        nc.vector.tensor_copy(bupsel[:, 0:512], ps_M[0][0:1, 0:512])
        nc.vector.tensor_copy(bupsel[:, 512:1024], ps_M[1][0:1, 0:512])
        nc.vector.drain()
        ctx.wait("v", ev_wres)
        for c in range(NCH):
            nc.vector.tensor_scalar(out=wadn_sel[c][:], in0=wadn_all[c][:, 0:64],
                                    scalar1=ohc[:, 0:1], scalar2=None, op0=ALU.mult)
            nc.vector.drain()
            for e in range(1, NE):
                nc.vector.scalar_tensor_tensor(out=wadn_sel[c][:],
                                               in0=wadn_all[c][:, e * 64:(e + 1) * 64],
                                               scalar=ohc[:, e:e + 1], in1=wadn_sel[c][:],
                                               op0=ALU.mult, op1=ALU.add)
                nc.vector.drain()
        nc.vector.tensor_scalar(out=waup_sel[:], in0=waup_all[:, 0:1024],
                                scalar1=ohc[0:64, 0:1], scalar2=None, op0=ALU.mult)
        nc.vector.drain()
        for e in range(1, NE):
            nc.vector.scalar_tensor_tensor(out=waup_sel[:],
                                           in0=waup_all[:, e * 1024:(e + 1) * 1024],
                                           scalar=ohc[0:64, e:e + 1], in1=waup_sel[:],
                                           op0=ALU.mult, op1=ALU.add)
            nc.vector.drain()
        nc.vector.sem_inc(ctx.sems["V"], 1)
        ev_sel = ctx.inc("V", 1)
        ctx.barrier()

    with contextlib.ExitStack() as sfin:
        aT = sfin.enter_context(nc.sbuf_tensor("m_aT", [64, 4096], BF16))
        ps_a = sfin.enter_context(nc.psum_tensor("m_pa", [64, 512], F32))
        ps_y = [sfin.enter_context(nc.psum_tensor(f"m_py{r}", [128, 1024], F32)) for r in range(2)]
        cbufs = mk_phase_c_bufs(ctx, sfin)
        ps_a_free = None
        ps_y_free = [None, None]
        ev_done = None
        ctx.wait("t", ev_wres)
        for tb in range(NTB):
            tbs = slice(tb * 512, (tb + 1) * 512)
            ctx.wait("t", ps_a_free)
            for kc in range(NCH):
                mma = nc.tensor.matmul(ps_a[:], wadn_sel[kc][:], xn_dim[kc][:, tbs],
                                       start=(kc == 0), stop=(kc == NCH - 1),
                                       skip_group_check=True)
            mma.then_inc(ctx.sems["T"], 1)
            ev_ap = ctx.inc("T", 1)
            ctx.wait("s", ev_ap)
            nc.scalar.activation(out=aT[:, tbs], in_=ps_a[:], func=AF.Relu,
                                 bias=basel[:], scale=1.0).then_inc(ctx.sems["S"], 1)
            ps_a_free = ctx.inc("S", 1)
        ev_aT = ps_a_free
        for i in range(NT):
            yr = i % 2
            ctx.wait("t", ev_aT)
            ctx.wait("t", ps_y_free[yr])
            for half in range(2):
                hs = slice(half * 512, (half + 1) * 512)
                for kc in range(NCH):
                    nc.tensor.matmul(ps_y[yr][:, hs], xn_dim[kc][:, i * 128:(i + 1) * 128],
                                     wbase[kc][:, hs], start=(kc == 0), stop=False,
                                     skip_group_check=True)
                nc.tensor.matmul(ps_y[yr][:, hs], onesbf_r[:], bbase_r[:, hs],
                                 start=False, stop=False, skip_group_check=True)
                nc.tensor.matmul(ps_y[yr][:, hs], aT[:, i * 128:(i + 1) * 128],
                                 waup_sel[:, hs], start=False, stop=False,
                                 skip_group_check=True)
                mmy = nc.tensor.matmul(ps_y[yr][:, hs], onesbf_r[:], bupsel[:, hs],
                                       start=False, stop=True, skip_group_check=True)
            mmy.then_inc(ctx.sems["T"], 1)
            ev_y = ctx.inc("T", 1)
            ev_done, yfree = emit_phase_c(ctx, sfin, io["x_in"], io["x_out"],
                                          ps_y[yr], ev_y, i, cbufs)
            ps_y_free[yr] = yfree
    return ev_done


def l1f_weight_maps(inputs):
    W_base = np.asarray(inputs["L1_ffn_W_base"])
    b_base = np.asarray(inputs["L1_ffn_b_base"])
    Wd1 = np.asarray(inputs["L1_ffn_Wd1"])
    bd1 = np.asarray(inputs["L1_ffn_bd1"])
    Wd2 = np.asarray(inputs["L1_ffn_Wd2"])
    bd2 = np.asarray(inputs["L1_ffn_bd2"])
    Wa_down = np.asarray(inputs["L1_ffn_Wa_down"])
    ba_down = np.asarray(inputs["L1_ffn_ba_down"])
    Wa_up = np.asarray(inputs["L1_ffn_Wa_up"])
    ba_up = np.asarray(inputs["L1_ffn_ba_up"])
    wadn = Wa_down.transpose(1, 0, 2).reshape(D, 8 * 64)
    waup = Wa_up.transpose(1, 0, 2).reshape(64, 8 * 1024)
    return {
        "w_base2": np.ascontiguousarray(W_base, np.float32),
        "w_bbase": np.ascontiguousarray(b_base.reshape(1, -1), np.float32),
        "w_d1": np.ascontiguousarray(Wd1, np.float32),
        "w_bd1": np.ascontiguousarray(bd1, np.float32),
        "w_d2": np.ascontiguousarray(Wd2, np.float32),
        "w_bd2": np.ascontiguousarray(bd2, np.float32),
        "w_bd2t": np.ascontiguousarray(bd2.T, np.float32),
        "w_adn": np.ascontiguousarray(wadn, np.float32),
        "w_badn": np.ascontiguousarray(ba_down, np.float32),
        "w_aup": np.ascontiguousarray(waup, np.float32),
        "w_baup": np.ascontiguousarray(ba_up, np.float32),
        "w_iota8": np.arange(8, dtype=np.float32).reshape(1, 8),
    }


DEBUG_TAPS = []


def build_program(blocks):
    nc = bass.Bass("TRN2", target_bir_lowering=False, debug=False)
    x_in0 = nc.dram_tensor("xin", [SEQ, D], F32, kind="ExternalInput").ap()
    x_out_f = nc.dram_tensor("xout", [SEQ, D], F32, kind="ExternalOutput").ap()
    ident = nc.dram_tensor("ident", [128, 128], F32, kind="ExternalInput").ap()
    scratch = [nc.dram_tensor(f"xs{i}", [SEQ, D], F32, kind="Internal").ap()
               for i in range(2)]

    W = {"ident": ident}
    W["masks"] = nc.dram_tensor("masks", [4, 128, 512], F32, kind="ExternalInput").ap()
    W["maskb"] = nc.dram_tensor("maskb", [4, 128, 512], F32, kind="ExternalInput").ap()
    for nm, shp in DEBUG_TAPS:
        W[nm] = nc.dram_tensor(nm, shp, F32, kind="ExternalOutput").ap()
    if "l0a" in blocks:
        W["wqkv"] = nc.dram_tensor("w_qkv", [D, 3072], F32, kind="ExternalInput").ap()
        W["bqk"] = nc.dram_tensor("w_bqk", [2, D, 1], F32, kind="ExternalInput").ap()
        W["bvrow"] = nc.dram_tensor("w_bvrow", [1, D], F32, kind="ExternalInput").ap()
        W["woext"] = nc.dram_tensor("w_oext", [64, 16 * 1024], F32, kind="ExternalInput").ap()
        W["borow"] = nc.dram_tensor("w_borow", [1, D], F32, kind="ExternalInput").ap()
        W["qt_s"] = nc.dram_tensor("qt_s", [D, SEQ], BF16, kind="Internal").ap()
        W["kt_s"] = nc.dram_tensor("kt_s", [D, SEQ], BF16, kind="Internal").ap()
        W["v_s"] = nc.dram_tensor("v_s", [NT, 128, 1056], BF16, kind="Internal").ap()
    if "l1a" in blocks:
        W["wqkvT1"] = nc.dram_tensor("w_qkvT1", [D, 768], F32, kind="ExternalInput").ap()
        W["woT"] = nc.dram_tensor("w_oT", [256, D], F32, kind="ExternalInput").ap()
    if "l1f" in blocks:
        W["wbase"] = nc.dram_tensor("w_base2", [D, D], F32, kind="ExternalInput").ap()
        W["bbase"] = nc.dram_tensor("w_bbase", [1, D], F32, kind="ExternalInput").ap()
        W["wd1"] = nc.dram_tensor("w_d1", [8, D, 256], F32, kind="ExternalInput").ap()
        W["bd1"] = nc.dram_tensor("w_bd1", [8, 256], F32, kind="ExternalInput").ap()
        W["wd2"] = nc.dram_tensor("w_d2", [8, 256, D], F32, kind="ExternalInput").ap()
        W["bd2"] = nc.dram_tensor("w_bd2", [8, D], F32, kind="ExternalInput").ap()
        W["bd2t"] = nc.dram_tensor("w_bd2t", [D, 8], F32, kind="ExternalInput").ap()
        W["wadn"] = nc.dram_tensor("w_adn", [D, 512], F32, kind="ExternalInput").ap()
        W["badn"] = nc.dram_tensor("w_badn", [8, 64], F32, kind="ExternalInput").ap()
        W["waup"] = nc.dram_tensor("w_aup", [64, 8192], F32, kind="ExternalInput").ap()
        W["baup"] = nc.dram_tensor("w_baup", [8, D], F32, kind="ExternalInput").ap()
        W["iota8"] = nc.dram_tensor("w_iota8", [1, 8], F32, kind="ExternalInput").ap()
    if "l0f" in blocks:
        W["winrep"] = nc.dram_tensor("w_winrep", [D, D], F32, kind="ExternalInput").ap()
        W["binrep"] = nc.dram_tensor("w_binrep", [D, 1], F32, kind="ExternalInput").ap()
        W["w1f"] = nc.dram_tensor("w_w1f", [D, 1], F32, kind="ExternalInput").ap()
        W["b1f"] = nc.dram_tensor("w_b1f", [D, 1], F32, kind="ExternalInput").ap()
        W["w2bd"] = nc.dram_tensor("w_w2bd", [NCH, 128, 64], F32, kind="ExternalInput").ap()
        W["b2"] = nc.dram_tensor("w_b2", [64, 1], F32, kind="ExternalInput").ap()
        W["wout"] = nc.dram_tensor("w_wout", [64, D], F32, kind="ExternalInput").ap()
        W["boutrow"] = nc.dram_tensor("w_bout", [1, D], F32, kind="ExternalInput").ap()

    orig_sb = nc.sbuf_tensor
    orig_ps = nc.psum_tensor
    uid = {"n": 0}

    def _sb(name, shape, dtype, **kw):
        return orig_sb(f"{name}_u{uid['n']}", shape, dtype, **kw)

    def _ps(name, shape, dtype, **kw):
        return orig_ps(f"{name}_u{uid['n']}", shape, dtype, **kw)

    nc.sbuf_tensor = _sb
    nc.psum_tensor = _ps

    with contextlib.ExitStack() as stack:
        ctx = Ctx(nc, stack)
        ctx.setup_consts(stack)
        n = len(blocks)
        for bi, b in enumerate(blocks):
            uid["n"] = bi + 1
            x_in = x_in0 if bi == 0 else scratch[(bi - 1) % 2]
            x_out = x_out_f if bi == n - 1 else scratch[bi % 2]
            io = {"x_in": x_in, "x_out": x_out}
            with contextlib.ExitStack() as bst:
                if b == "l0f":
                    emit_l0_ffn(ctx, bst, io, W)
                elif b == "l0a":
                    emit_l0_attn(ctx, bst, io, W)
                elif b == "l1a":
                    emit_l1_attn(ctx, bst, io, W)
                elif b == "l1f":
                    emit_l1_ffn(ctx, bst, io, W)
                else:
                    BLOCK_EMITTERS[b](ctx, bst, io, W)
                ctx.barrier()
    return nc


def l0f_weight_maps(inputs):
    Win = np.asarray(inputs["L0_ffn_Win"])
    bin_ = np.asarray(inputs["L0_ffn_bin_"])
    W1 = np.asarray(inputs["L0_ffn_W1"])
    b1 = np.asarray(inputs["L0_ffn_b1"])
    W2 = np.asarray(inputs["L0_ffn_W2"])
    b2 = np.asarray(inputs["L0_ffn_b2"])
    Wout = np.asarray(inputs["L0_ffn_Wout"])
    bout = np.asarray(inputs["L0_ffn_bout"])
    winrep = np.repeat(Win, 16, axis=1)
    binrep = np.repeat(bin_, 16)
    w1f = W1.reshape(-1)
    b1f = b1.reshape(-1)
    w2bd = np.zeros((NCH, 128, 64), np.float32)
    mm = np.arange(1024) // 16
    ii = np.arange(1024) % 16
    w2bd[np.arange(1024) // 128, np.arange(1024) % 128, mm] = W2[mm, ii]
    return {
        "w_winrep": np.ascontiguousarray(winrep, np.float32),
        "w_binrep": np.ascontiguousarray(binrep.reshape(-1, 1), np.float32),
        "w_w1f": np.ascontiguousarray(w1f.reshape(-1, 1), np.float32),
        "w_b1f": np.ascontiguousarray(b1f.reshape(-1, 1), np.float32),
        "w_w2bd": w2bd,
        "w_b2": np.ascontiguousarray(b2.reshape(-1, 1), np.float32),
        "w_wout": np.ascontiguousarray(Wout, np.float32),
        "w_bout": np.ascontiguousarray(bout.reshape(1, -1), np.float32),
    }


ALL_BLOCKS = ("l0a", "l0f", "l1a", "l1f")


def kernel(**inputs):
    x = np.asarray(inputs["L0_attn_x"], dtype=np.float32)
    wm = {}
    wm.update(l0a_weight_maps(inputs))
    wm.update(l0f_weight_maps(inputs))
    wm.update(l1a_weight_maps(inputs))
    wm.update(l1f_weight_maps(inputs))
    consts = {"ident": np.eye(128, dtype=np.float32), "masks": make_masks(),
              "maskb": make_maskb()}
    nc = build_program(ALL_BLOCKS)
    in_maps = [{"xin": np.ascontiguousarray(x[b]), **consts, **wm} for b in range(8)]
    res = run_bass_kernel_spmd(nc, in_maps, core_ids=list(range(8)))
    return np.stack([np.asarray(r["xout"], dtype=np.float32) for r in res.results])


# version 20
# speedup vs baseline: 1.1551x; 1.1551x over previous
import contextlib
import numpy as np
import concourse.bass as bass
import concourse.mybir as mybir
from concourse.bass_utils import run_bass_kernel_spmd

F32 = mybir.dt.float32
BF16 = mybir.dt.bfloat16
AF = mybir.ActivationFunctionType
ALU = mybir.AluOpType
AX = mybir.AxisListType

PARANOID = False
SEQ = 4096
D = 1024
NT = SEQ // 128
NCH = D // 128
NTB = SEQ // 512
LN_EPS = 1e-5


class Ctx:

    ENGS = ("t", "v", "s", "g", "y")

    def __init__(self, nc, stack):
        self.nc = nc
        self.eng = {
            "t": nc.tensor, "v": nc.vector, "s": nc.scalar,
            "g": nc.gpsimd, "y": nc.sync,
        }
        self.sems = {}
        self.cnt = {}
        self.waited = {}
        for name in ("T", "V", "S", "G", "DY", "DG", "DS", "B"):
            self.sems[name] = stack.enter_context(nc.semaphore("sem" + name))
            self.cnt[name] = 0

    def setup_consts(self, st):
        nc = self.nc
        self.cconst = st.enter_context(nc.sbuf_tensor("cc_pool", [128, 4], F32))
        nc.vector.memset(self.cconst[:, 0:1], LN_EPS)
        nc.vector.memset(self.cconst[:, 1:2], 0.0)
        nc.vector.memset(self.cconst[:, 2:3], 1.0).then_inc(self.sems["V"], 1)
        self.const_evt = self.inc("V", 1)
        self.eps_col = self.cconst[:, 0:1]

    def inc(self, sem, k):
        self.cnt[sem] += k
        return (sem, self.cnt[sem])

    def wait(self, ename, evt):
        if evt is None:
            return
        if isinstance(evt, (list,)):
            for e in evt:
                self.wait(ename, e)
            return
        sem, c = evt
        if c <= 0:
            return
        key = (ename, sem)
        if self.waited.get(key, 0) >= c:
            return
        self.eng[ename].wait_ge(self.sems[sem], c)
        self.waited[key] = c

    def barrier(self):
        self.wait("y", ("DY", self.cnt["DY"]))
        self.wait("g", ("DG", self.cnt["DG"]))
        self.wait("s", ("DS", self.cnt["DS"]))
        self.cnt["B"] += len(self.ENGS)
        tgt = self.cnt["B"]
        for en in self.ENGS:
            self.eng[en].sem_inc(self.sems["B"], 1)
        for en in self.ENGS:
            self.eng[en].wait_ge(self.sems["B"], tgt)
            self.waited[(en, "B")] = tgt


def emit_phase_a(ctx, st, x_hbm, xn_dim, ident_evt, ident_bf, xn_tok=None,
                 stats_sb=None, ring=2, tr_banks=2):
    nc = ctx.nc
    xin = [st.enter_context(nc.sbuf_tensor(f"pa_x{i}", [128, 1024], F32)) for i in range(ring)]
    own_tok = xn_tok is None
    if own_tok:
        xn_tok = [st.enter_context(nc.sbuf_tensor(f"pa_n{i}", [128, 1024], BF16)) for i in range(ring)]
    bn12 = st.enter_context(nc.sbuf_tensor("pa_bn", [128, 12], F32))
    stats = stats_sb if stats_sb is not None else st.enter_context(
        nc.sbuf_tensor("pa_st", [128, 4 * NT], F32))
    pstr = [st.enter_context(nc.psum_tensor(f"pa_tr{i}", [128, 128], BF16)) for i in range(tr_banks)]

    xin_free = [None] * ring
    tok_free = [None] * ring
    tr_free = [None, None]
    col_done = [None] * NT

    for i in range(NT):
        sl = i % ring
        ctx.wait("y", xin_free[sl])
        nc.sync.dma_start(out=xin[sl][:], in_=x_hbm[i * 128:(i + 1) * 128, :]
                          ).then_inc(ctx.sems["DY"], 16)
        ev_load = ctx.inc("DY", 16)

        ctx.wait("v", ev_load)
        nc.vector.bn_stats(bn12[:, 0:6], xin[sl][:, 0:512])
        nc.vector.bn_stats(bn12[:, 6:12], xin[sl][:, 512:1024])
        nc.vector.drain()
        nc.vector.bn_aggr(stats[:, 4 * i:4 * i + 2], bn12[:]).then_inc(ctx.sems["V"], 1)
        ev_mv = ctx.inc("V", 1)

        ctx.wait("s", ev_mv)
        ctx.wait("s", ctx.const_evt)
        nc.scalar.activation(out=stats[:, 4 * i + 2:4 * i + 3],
                             in_=stats[:, 4 * i + 1:4 * i + 2], func=AF.Ln,
                             bias=ctx.eps_col, scale=1.0)
        nc.scalar.drain()
        nc.scalar.activation(out=stats[:, 4 * i + 2:4 * i + 3],
                             in_=stats[:, 4 * i + 2:4 * i + 3], func=AF.Exp,
                             scale=-0.5).then_inc(ctx.sems["S"], 1)
        ev_rstd = ctx.inc("S", 1)

        ctx.wait("v", ev_rstd)
        nc.vector.tensor_scalar(out=stats[:, 4 * i + 3:4 * i + 4],
                                in0=stats[:, 4 * i:4 * i + 1],
                                scalar1=stats[:, 4 * i + 2:4 * i + 3], scalar2=-1.0,
                                op0=ALU.mult, op1=ALU.mult)
        nc.vector.drain()
        ctx.wait("v", tok_free[sl])
        dst_tok = xn_tok[sl] if own_tok else xn_tok[i]
        nc.vector.tensor_scalar(out=dst_tok[:], in0=xin[sl][:],
                                scalar1=stats[:, 4 * i + 2:4 * i + 3],
                                scalar2=stats[:, 4 * i + 3:4 * i + 4],
                                op0=ALU.mult, op1=ALU.add).then_inc(ctx.sems["V"], 1)
        ev_xn = ctx.inc("V", 1)
        xin_free[sl] = ev_xn

        ev_last_v = None
        ev_last_s = None
        for c in range(NCH):
            pb = c % tr_banks
            ctx.wait("t", ev_xn)
            ctx.wait("t", ident_evt)
            ctx.wait("t", tr_free[pb])
            nc.tensor.transpose(pstr[pb][:], dst_tok[:, c * 128:(c + 1) * 128],
                                ident_bf[:]).then_inc(ctx.sems["T"], 1)
            ev_tr = ctx.inc("T", 1)
            if pb == 0:
                ctx.wait("v", ev_tr)
                nc.vector.tensor_copy(xn_dim[c][:, i * 128:(i + 1) * 128], pstr[pb][:]
                                      ).then_inc(ctx.sems["V"], 1)
                ev_evac = ctx.inc("V", 1)
                ev_last_v = ev_evac
            else:
                ctx.wait("s", ev_tr)
                nc.scalar.copy(xn_dim[c][:, i * 128:(i + 1) * 128], pstr[pb][:]
                               ).then_inc(ctx.sems["S"], 1)
                ev_evac = ctx.inc("S", 1)
                ev_last_s = ev_evac
            tr_free[pb] = ev_evac
        col_done[i] = [ev_last_v, ev_last_s]
        tok_free[sl] = ev_tr

    return {"col_done": col_done, "stats": stats}


def mk_phase_c_bufs(ctx, st):
    nc = ctx.nc
    return {
        "xrel": [st.enter_context(nc.sbuf_tensor(f"pc_xr{i}", [128, 1024], F32)) for i in range(2)],
        "bn12c": st.enter_context(nc.sbuf_tensor("pc_bn", [128, 12], F32)),
        "cstats": st.enter_context(nc.sbuf_tensor("pc_st", [128, 8], F32)),
        "xrel_free": [None] * 2,
        "cst_free": [None, None],
    }


def emit_phase_c(ctx, st, x_hbm_in, x_hbm_out, y_psum, y_evt, i, bufs):
    nc = ctx.nc
    xrel = bufs["xrel"]
    bn12, cst = bufs["bn12c"], bufs["cstats"]
    sl = i % 2
    r = i % 2
    co = 4 * r
    ctx.wait("y", bufs["xrel_free"][sl])
    nc.sync.dma_start(out=xrel[sl][:], in_=x_hbm_in[i * 128:(i + 1) * 128, :]
                      ).then_inc(ctx.sems["DY"], 16)
    ev_xload = ctx.inc("DY", 16)

    ctx.wait("v", y_evt)
    nc.vector.bn_stats(bn12[:, 0:6], y_psum[:, 0:512])
    nc.vector.bn_stats(bn12[:, 6:12], y_psum[:, 512:1024])
    nc.vector.drain()
    ctx.wait("v", bufs["cst_free"][r])
    nc.vector.bn_aggr(cst[:, co:co + 2], bn12[:]).then_inc(ctx.sems["V"], 1)
    ev_mv = ctx.inc("V", 1)

    ctx.wait("s", ev_mv)
    ctx.wait("s", ctx.const_evt)
    nc.scalar.activation(out=cst[:, co + 2:co + 3], in_=cst[:, co + 1:co + 2],
                         func=AF.Ln, bias=ctx.eps_col, scale=1.0)
    nc.scalar.drain()
    nc.scalar.activation(out=cst[:, co + 2:co + 3], in_=cst[:, co + 2:co + 3],
                         func=AF.Exp, scale=-0.5).then_inc(ctx.sems["S"], 1)
    ev_rstd = ctx.inc("S", 1)
    bufs["cst_free"][r] = ev_rstd

    ctx.wait("v", ev_rstd)
    nc.vector.tensor_scalar(out=cst[:, co + 3:co + 4], in0=cst[:, co:co + 1],
                            scalar1=cst[:, co + 2:co + 3], scalar2=-1.0,
                            op0=ALU.mult, op1=ALU.mult).then_inc(ctx.sems["V"], 1)
    ev_nmr = ctx.inc("V", 1)
    ctx.wait("s", ev_nmr)
    nc.scalar.activation(out=y_psum[:], in_=y_psum[:], func=AF.Identity,
                         scale=cst[:, co + 2:co + 3], bias=cst[:, co + 3:co + 4]
                         ).then_inc(ctx.sems["S"], 1)
    ev_yn = ctx.inc("S", 1)
    ctx.wait("v", ev_yn)
    ctx.wait("v", ev_xload)
    nc.vector.tensor_tensor(out=xrel[sl][:], in0=xrel[sl][:],
                            in1=y_psum[:], op=ALU.add).then_inc(ctx.sems["V"], 1)
    ev_x = ctx.inc("V", 1)

    ctx.wait("y", ev_x)
    nc.sync.dma_start(out=x_hbm_out[i * 128:(i + 1) * 128, :], in_=xrel[sl][:]
                      ).then_inc(ctx.sems["DY"], 16)
    ev_done = ctx.inc("DY", 16)
    bufs["xrel_free"][sl] = ev_done
    return ev_done, ev_x


def emit_l0_ffn(ctx, st, io, W):
    nc = ctx.nc
    xn_dim = [st.enter_context(nc.sbuf_tensor(f"f_xnd{c}", [128, 4096], BF16)) for c in range(NCH)]
    ident_bf = st.enter_context(nc.sbuf_tensor("f_id", [128, 128], BF16))
    winrep = [st.enter_context(nc.sbuf_tensor(f"f_wr{c}", [128, 1024], BF16)) for c in range(NCH)]
    hb = st.enter_context(nc.sbuf_tensor("f_hb", [128, 3 * NCH], F32))
    hbias = st.enter_context(nc.sbuf_tensor("f_hbias", [128, NCH], F32))
    w2bd = [st.enter_context(nc.sbuf_tensor(f"f_w2{c}", [128, 64], BF16)) for c in range(NCH)]
    b2c = st.enter_context(nc.sbuf_tensor("f_b2", [64, 1], F32))
    wout = st.enter_context(nc.sbuf_tensor("f_wo", [64, 1024], BF16))
    bout = st.enter_context(nc.sbuf_tensor("f_bo", [1, 1024], BF16))
    onesc = st.enter_context(nc.sbuf_tensor("f_ones", [1, 128], BF16))
    hsb = [st.enter_context(nc.sbuf_tensor(f"f_h{i}", [128, 512], BF16)) for i in range(2)]
    gsb = [st.enter_context(nc.sbuf_tensor(f"f_g{i}", [64, 512], BF16)) for i in range(2)]

    def gdma(dst, src):
        nc.gpsimd.dma_start(out=dst, in_=src).then_inc(ctx.sems["DG"], 16)
        return ctx.inc("DG", 16)

    ev_id = gdma(ident_bf[:], W["ident"][:, :])
    for c in range(NCH):
        ev_wr = gdma(winrep[c][:], W["winrep"][c * 128:(c + 1) * 128, :])
    for c in range(NCH):
        gdma(hb[:, 3 * c:3 * c + 1], W["binrep"][c * 128:(c + 1) * 128, :])
        gdma(hb[:, 3 * c + 1:3 * c + 2], W["w1f"][c * 128:(c + 1) * 128, :])
        ev_hb = gdma(hb[:, 3 * c + 2:3 * c + 3], W["b1f"][c * 128:(c + 1) * 128, :])
    for c in range(NCH):
        ev_w2 = gdma(w2bd[c][:], W["w2bd"][c, :, :])
    gdma(b2c[:], W["b2"][:, :])
    gdma(wout[:], W["wout"][:, :])
    ev_wo = gdma(bout[:], W["boutrow"][:, :])

    ctx.wait("v", ev_hb)
    for c in range(NCH):
        nc.vector.tensor_scalar(out=hbias[:, c:c + 1], in0=hb[:, 3 * c:3 * c + 1],
                                scalar1=hb[:, 3 * c + 1:3 * c + 2],
                                scalar2=hb[:, 3 * c + 2:3 * c + 3],
                                op0=ALU.mult, op1=ALU.add)
    nc.vector.memset(onesc[:], 1.0).then_inc(ctx.sems["V"], 1)
    ev_hbias = ctx.inc("V", 1)

    pa = emit_phase_a(ctx, st, io["x_in"], xn_dim, ev_id, ident_bf, ring=4, tr_banks=1)

    ps_s = [st.enter_context(nc.psum_tensor(f"f_ps{i}", [128, 512], F32)) for i in range(2)]
    ps_g = st.enter_context(nc.psum_tensor("f_pg", [64, 512], F32))
    ps_y = [st.enter_context(nc.psum_tensor(f"f_py{i}", [128, 1024], F32)) for i in range(2)]
    cbufs = mk_phase_c_bufs(ctx, st)

    ps_s_free = [None, None]
    ps_g_free = None
    ps_y_free = [None, None]
    h_free = [None, None]
    g_free = [None, None]
    ev_done = None

    def emit_block_tb(tb):
        tbs = slice(tb * 512, (tb + 1) * 512)
        ev_cols = pa["col_done"][4 * tb + 3]
        ev_g = None
        ev_srep_l = [None] * NCH

        def emit_srep(c):
            pb = c % 2
            ctx.wait("t", ev_cols)
            ctx.wait("t", ev_wr)
            ctx.wait("t", ps_s_free[pb])
            for kc in range(NCH):
                mm = nc.tensor.matmul(ps_s[pb][:], winrep[kc][:, c * 128:(c + 1) * 128],
                                      xn_dim[kc][:, tbs], start=(kc == 0),
                                      stop=(kc == NCH - 1), skip_group_check=True)
            mm.then_inc(ctx.sems["T"], 1)
            return ctx.inc("T", 1)

        def emit_hg(c):
            nonlocal ev_g
            pb = c % 2
            ctx.wait("s", ev_srep_l[c])
            ctx.wait("s", ev_hbias)
            ctx.wait("s", h_free[pb])
            nc.scalar.activation(out=hsb[pb][:], in_=ps_s[pb][:], func=AF.Relu,
                                 scale=hb[:, 3 * c + 1:3 * c + 2],
                                 bias=hbias[:, c:c + 1]).then_inc(ctx.sems["S"], 1)
            ev_h = ctx.inc("S", 1)
            ps_s_free[pb] = ev_h
            ctx.wait("t", ev_h)
            ctx.wait("t", ev_w2)
            if c == 0:
                ctx.wait("t", ps_g_free)
            nc.tensor.matmul(ps_g[:], w2bd[c][:], hsb[pb][:], start=(c == 0),
                             stop=(c == NCH - 1), skip_group_check=True
                             ).then_inc(ctx.sems["T"], 1)
            ev_g = ctx.inc("T", 1)
            h_free[pb] = ev_g

        for c in range(NCH):
            ev_srep_l[c] = emit_srep(c)
            if c >= 1:
                emit_hg(c - 1)
        emit_hg(NCH - 1)
        gb = tb % 2
        ctx.wait("v", ev_g)
        ctx.wait("v", g_free[gb])
        nc.vector.tensor_scalar(out=gsb[gb][:], in0=ps_g[:], scalar1=b2c[:],
                                scalar2=None, op0=ALU.add).then_inc(ctx.sems["V"], 1)
        ev_gt = ctx.inc("V", 1)
        return ev_gt, gb

    def emit_out_tb(tb, ev_gt, gb):
        nonlocal ps_g_free, ev_done
        ps_g_free = ev_gt
        for u in range(4):
            i = 4 * tb + u
            yr = i % 2
            ctx.wait("t", ev_gt)
            ctx.wait("t", ev_wo)
            ctx.wait("t", ps_y_free[yr])
            for half in range(2):
                nc.tensor.matmul(ps_y[yr][:, half * 512:(half + 1) * 512],
                                 gsb[gb][:, u * 128:(u + 1) * 128],
                                 wout[:, half * 512:(half + 1) * 512],
                                 start=True, stop=False, skip_group_check=True)
                mm = nc.tensor.matmul(ps_y[yr][:, half * 512:(half + 1) * 512],
                                      onesc[:, 0:128],
                                      bout[:, half * 512:(half + 1) * 512],
                                      start=False, stop=True, skip_group_check=True)
            mm.then_inc(ctx.sems["T"], 1)
            ev_y = ctx.inc("T", 1)
            ev_done, yfree = emit_phase_c(ctx, st, io["x_in"], io["x_out"],
                                          ps_y[yr], ev_y, i, cbufs)
            ps_y_free[yr] = yfree
            if u == 3:
                g_free[gb] = yfree

    prev = None
    for tb in range(NTB):
        r = emit_block_tb(tb)
        if prev is not None:
            emit_out_tb(prev[0], *prev[1])
        prev = (tb, r)
    emit_out_tb(prev[0], *prev[1])
    return ev_done


def flush_divide(pending, ctx, nc, ps_s, rden, zdiv, z_free,
                 s_free, zdiv_done, ones_r2, ev_ones2, ev_opl=None):
    h0, z_t, ev_rd = pending[0]
    for p in range(2):
        h = h0 + p
        ctx.wait("v", ev_opl)
        ctx.wait("t", ev_rd)
        ctx.wait("t", ev_ones2)
        ctx.wait("t", s_free[1])
        nc.tensor.matmul(ps_s[1][:, 512 * p:512 * p + 512], ones_r2[:],
                         rden[0:1, 512 * p:512 * p + 512],
                         start=True, stop=True, skip_group_check=True
                         ).then_inc(ctx.sems["T"], 1)
        ev_bc = ctx.inc("T", 1)
        pending[1] = ev_bc
        ctx.wait("v", ev_bc)
        nc.vector.tensor_copy(zdiv[h][:], z_t[0:64, 512 * p:512 * p + 512])
        nc.vector.drain()
        nc.vector.tensor_tensor(out=zdiv[h][:], in0=zdiv[h][:],
                                in1=ps_s[1][0:64, 512 * p:512 * p + 512], op=ALU.mult
                                ).then_inc(ctx.sems["V"], 1)
        ev_zd = ctx.inc("V", 1)
        z_free[id(z_t)][p] = ev_zd
        s_free[1] = ev_zd
        zdiv_done[h] = ev_zd
    pending[0] = None


def emit_l0_attn(ctx, st, io, W):
    nc = ctx.nc
    NH = 16

    with contextlib.ExitStack() as sp:
        xn_dim = [sp.enter_context(nc.sbuf_tensor(f"o_xnd{c}", [128, 4096], BF16)) for c in range(NCH)]
        ident_bf = sp.enter_context(nc.sbuf_tensor("o_id", [128, 128], BF16))
        wqkv = [sp.enter_context(nc.sbuf_tensor(f"o_wq{c}", [128, 3072], BF16)) for c in range(NCH)]
        bqk = sp.enter_context(nc.sbuf_tensor("o_bqk", [128, 16], F32))
        bvr = sp.enter_context(nc.sbuf_tensor("o_bv", [1, 1024], BF16))
        ones_r = sp.enter_context(nc.sbuf_tensor("o_1r", [1, 128], BF16))
        qk_stage = [sp.enter_context(nc.sbuf_tensor(f"o_qs{i}", [128, 2048], BF16)) for i in range(2)]
        v_stage = [sp.enter_context(nc.sbuf_tensor(f"o_vs{i}", [128, 1040], BF16)) for i in range(2)]

        def gdma(dst, src):
            nc.gpsimd.dma_start(out=dst, in_=src).then_inc(ctx.sems["DG"], 16)
            return ctx.inc("DG", 16)

        ev_id = gdma(ident_bf[:], W["ident"][:, :])
        for c in range(NCH):
            ev_w = gdma(wqkv[c][:], W["wqkv"][c * 128:(c + 1) * 128, :])
        for q in range(2):
            for c in range(NCH):
                ev_b = gdma(bqk[:, 8 * q + c:8 * q + c + 1], W["bqk"][q, c * 128:(c + 1) * 128, :])
        ev_bv = gdma(bvr[:], W["bvrow"][:, :])
        nc.vector.memset(ones_r[:], 1.0).then_inc(ctx.sems["V"], 1)
        ev_ones = ctx.inc("V", 1)

        pa = emit_phase_a(ctx, sp, io["x_in"], xn_dim, ev_id, ident_bf, ring=6)

        ps_p = [sp.enter_context(nc.psum_tensor(f"o_pp{i}", [128, 512], F32)) for i in range(2)]
        ps_v = sp.enter_context(nc.psum_tensor("o_pv", [128, 1024], F32))
        pp_free = [None, None]
        st_free = [None] * 4
        vstg_free = [None, None]
        pv_free = None

        for tb in range(NTB):
            tbs = slice(tb * 512, (tb + 1) * 512)
            ev_cols = pa["col_done"][4 * tb + 3]
            for oc in range(16):
                r = oc % 2
                sl = (oc // 4) % 2
                so = oc % 4
                ctx.wait("t", ev_cols)
                ctx.wait("t", ev_w)
                ctx.wait("t", pp_free[r])
                for kc in range(NCH):
                    mm = nc.tensor.matmul(ps_p[r][:],
                                          wqkv[kc][:, oc * 128:(oc + 1) * 128],
                                          xn_dim[kc][:, tbs], start=(kc == 0),
                                          stop=(kc == NCH - 1), skip_group_check=True)
                mm.then_inc(ctx.sems["T"], 1)
                ev_p = ctx.inc("T", 1)
                ctx.wait("v", ev_p)
                ctx.wait("v", ev_b)
                if so == 0:
                    ctx.wait("v", st_free[sl])
                nc.vector.tensor_scalar(out=qk_stage[sl][:, so * 512:(so + 1) * 512],
                                        in0=ps_p[r][:],
                                        scalar1=bqk[:, oc:oc + 1], scalar2=None,
                                        op0=ALU.add).then_inc(ctx.sems["V"], 1)
                ev_st = ctx.inc("V", 1)
                pp_free[r] = ev_st
                if so == 3:
                    ctx.wait("y", ev_st)
                    dst = W["qt_s"] if oc < 8 else W["kt_s"]
                    base = (oc - 3) % 8
                    nc.sync.dma_start(
                        out=dst.rearrange("(c p) q -> p c q", p=128)[:, base:base + 4, tbs],
                        in_=qk_stage[sl][:].rearrange("p (c q) -> p c q", q=512)
                        ).then_inc(ctx.sems["DY"], 16)
                    st_free[sl] = ctx.inc("DY", 16)

        for i in range(NT):
            vs = i % 2
            ctx.wait("t", pa["col_done"][i])
            ctx.wait("t", ev_bv)
            ctx.wait("t", ev_ones)
            ctx.wait("t", pv_free)
            for half in range(2):
                for kc in range(NCH):
                    nc.tensor.matmul(ps_v[:, half * 512:(half + 1) * 512],
                                     xn_dim[kc][:, i * 128:(i + 1) * 128],
                                     wqkv[kc][:, 2048 + half * 512:2048 + (half + 1) * 512],
                                     start=(kc == 0), stop=False, skip_group_check=True)
                mm = nc.tensor.matmul(ps_v[:, half * 512:(half + 1) * 512], ones_r[:],
                                      bvr[:, half * 512:(half + 1) * 512],
                                      start=False, stop=True, skip_group_check=True)
            mm.then_inc(ctx.sems["T"], 1)
            ev_vp = ctx.inc("T", 1)
            ctx.wait("v", ev_vp)
            ctx.wait("v", vstg_free[vs])
            for half in range(2):
                nc.vector.tensor_copy(
                    v_stage[vs][:].rearrange("p (h u) -> p h u", u=65)[:, 8 * half:8 * half + 8, 0:64],
                    ps_v[:, half * 512:(half + 1) * 512].rearrange("p (h u) -> p h u", u=64))
            nc.vector.memset(v_stage[vs][:].rearrange("p (h u) -> p h u", u=65)[:, :, 64:65], 1.0
                             ).then_inc(ctx.sems["V"], 1)
            ev_vst = ctx.inc("V", 1)
            pv_free = ev_vst
            ctx.wait("y", ev_vst)
            nc.sync.dma_start(out=W["v_s"][i, :, :], in_=v_stage[vs][:]).then_inc(ctx.sems["DY"], 16)
            vstg_free[vs] = ctx.inc("DY", 16)
        ctx.barrier()

    with contextlib.ExitStack() as sf:
        kt_sb = [sf.enter_context(nc.sbuf_tensor(f"o_kt{c}", [128, 4096], BF16)) for c in range(NCH)]
        v_sb = [sf.enter_context(nc.sbuf_tensor(f"o_v{j}", [128, 1040], BF16)) for j in range(NT)]
        qt_sb = sf.enter_context(nc.sbuf_tensor("o_qt", [128, 4096], BF16))
        masks2 = sf.enter_context(nc.sbuf_tensor("o_mk", [128, 4 * 512], BF16))
        e_sb = [sf.enter_context(nc.sbuf_tensor(f"o_e{h}", [128, 1024], BF16)) for h in range(3)]
        zdiv = [sf.enter_context(nc.sbuf_tensor(f"o_z{h}", [64, 512], BF16)) for h in range(NH)]
        woext = sf.enter_context(nc.sbuf_tensor("o_wo", [64, NH * 1024], BF16))
        bo = sf.enter_context(nc.sbuf_tensor("o_bo", [1, 1024], BF16))
        ones_r2 = sf.enter_context(nc.sbuf_tensor("o_1r2", [1, 128], BF16))
        ident_bf2 = sf.enter_context(nc.sbuf_tensor("o_id2", [128, 128], BF16))
        rden = sf.enter_context(nc.sbuf_tensor("o_rd", [1, 1024], BF16))


        def gdma2(dst, src):
            nc.gpsimd.dma_start(out=dst, in_=src).then_inc(ctx.sems["DG"], 16)
            return ctx.inc("DG", 16)

        for c in range(NCH):
            ev_kt = gdma2(kt_sb[c][:], W["kt_s"][c * 128:(c + 1) * 128, :])
        for j in range(NT):
            ev_v = gdma2(v_sb[j][:], W["v_s"][j, :, :])
        for m in range(4):
            ev_mk = gdma2(masks2[:, 512 * m:512 * m + 512], W["maskb"][m, :, :])
        ev_id2 = gdma2(ident_bf2[:], W["ident"][:, :])
        ev_wo = gdma2(woext[:], W["woext"][:, :])
        ev_bo = gdma2(bo[:], W["borow"][:, :])
        nc.vector.memset(ones_r2[:], 1.0).then_inc(ctx.sems["V"], 1)
        ev_ones2 = ctx.inc("V", 1)
        if "dbg_qts" in W:
            ctx.wait("g", ev_v)
            nc.gpsimd.dma_start(out=W["dbg_qts"][:, :], in_=kt_sb[0][:]
                                ).then_inc(ctx.sems["DG"], 16)
            nc.gpsimd.dma_start(out=W["dbg_vs"][:, :], in_=v_sb[0][:]
                                ).then_inc(ctx.sems["DG"], 16)
            ctx.inc("DG", 32)

        ps_s = [sf.enter_context(nc.psum_tensor(f"o_ps{j}", [128, 1024], F32))
                for j in range(2)]
        ps_z = sf.enter_context(nc.psum_tensor("o_pz", [128, 1024], F32))
        ps_y = sf.enter_context(nc.psum_tensor("o_py", [128, 1024], F32))
        cbufs = mk_phase_c_bufs(ctx, sf)

        qt_load = None
        qt_free = None
        s_free = [None, None]
        e_free = [None, None, None]
        z_free = {id(ps_z): [None, None], id(ps_y): [None, None]}
        ps_y_free = None
        ev_done = None
        pending = [None, None]
        pend_op = []
        ev_outproj_box = [None]
        zdiv_done = [None] * NH

        qt_src = W["qt_s"].rearrange("(c p) q -> p c q", p=128)
        for qb in range(NTB):
            ctx.wait("y", qt_free)
            nc.sync.dma_start(out=qt_sb[:].rearrange("p (c q) -> p c q", q=512),
                              in_=qt_src[:, :, qb * 512:(qb + 1) * 512]
                              ).then_inc(ctx.sems["DY"], 16)
            qt_load = ctx.inc("DY", 16)
            np_t = 4 * qb + 4
            for hp in range(8):
                h0 = 2 * hp
                z_t = ps_z if hp % 2 == 0 else ps_y
                if hp in (0, 1) and pend_op:
                    todo, rest = (pend_op[:2], pend_op[2:]) if hp == 0 else (pend_op[:], [])
                    for (pqb, pu) in todo:
                        i = 4 * pqb + pu
                        ctx.wait("t", zdiv_done[NH - 1])
                        ctx.wait("t", ps_y_free)
                        for half in range(2):
                            for h in range(NH):
                                nc.tensor.matmul(ps_y[:, half * 512:(half + 1) * 512],
                                                 zdiv[h][:, pu * 128:(pu + 1) * 128],
                                                 woext[:, h * 1024 + half * 512:h * 1024 + (half + 1) * 512],
                                                 start=(h == 0), stop=False, skip_group_check=True)
                            mm = nc.tensor.matmul(ps_y[:, half * 512:(half + 1) * 512], ones_r2[:],
                                                  bo[:, half * 512:(half + 1) * 512],
                                                  start=False, stop=True, skip_group_check=True)
                        mm.then_inc(ctx.sems["T"], 1)
                        ev_yo = ctx.inc("T", 1)
                        ev_outproj_box[0] = ev_yo
                        ev_done, ps_y_free = emit_phase_c(ctx, sf, io["x_in"], io["x_out"],
                                                          ps_y, ev_yo, i, cbufs)
                    pend_op[:] = rest

                def emit_scores(j):
                    jr = j % 2
                    m = j - 4 * qb
                    diag = m >= 0
                    for p in range(2):
                        ctx.wait("t", qt_load)
                        ctx.wait("t", ev_kt)
                        ctx.wait("t", ev_mk)
                        ctx.wait("t", ev_id2)
                        ctx.wait("t", s_free[jr])
                        mm = nc.tensor.matmul(
                            ps_s[jr][:, 512 * p:512 * p + 512],
                            kt_sb[hp][64 * p:64 * p + 64, j * 128:(j + 1) * 128],
                            qt_sb[64 * p:64 * p + 64, hp * 512:(hp + 1) * 512],
                            start=True, stop=not diag, skip_group_check=True,
                            tile_position=(64 * p, 0))
                    if diag:
                        for p in range(2):
                            mm = nc.tensor.matmul(
                                ps_s[jr][:, 512 * p:512 * p + 512],
                                ident_bf2[:], masks2[:, 512 * m:512 * m + 512],
                                start=False, stop=True, skip_group_check=True)
                    mm.then_inc(ctx.sems["T"], 1)
                    return ctx.inc("T", 1)

                def emit_exp(j, ev_sc):
                    jr = j % 2
                    je = j % 3
                    ctx.wait("s", ev_sc)
                    ctx.wait("s", e_free[je])
                    nc.scalar.activation(out=e_sb[je][:], in_=ps_s[jr][:],
                                         func=AF.Exp, scale=0.125
                                         ).then_inc(ctx.sems["S"], 1)
                    ev_e = ctx.inc("S", 1)
                    s_free[jr] = ev_e
                    return ev_e

                def emit_av(j, ev_e):
                    je = j % 3
                    last_evs = []
                    for p in range(2):
                        h = h0 + p
                        ctx.wait("t", ev_e)
                        ctx.wait("t", ev_v)
                        if j == 0:
                            ctx.wait("t", z_free[id(z_t)][p])
                            if z_t is ps_y:
                                ctx.wait("t", ps_y_free)
                        nc.tensor.matmul(
                            z_t[0:65, 512 * p:512 * p + 512],
                            v_sb[j][:, 65 * h:65 * h + 65],
                            e_sb[je][:, 512 * p:512 * p + 512],
                            start=(j == 0), stop=(j == np_t - 1),
                            skip_group_check=True).then_inc(ctx.sems["T"], 1)
                        last_evs.append(ctx.inc("T", 1))
                    e_free[je] = last_evs[-1]
                    return last_evs

                ev_sc_l = [None] * (np_t + 1)
                ev_e_l = [None] * np_t
                ev_sc_l[0] = emit_scores(0)
                if pending[0] is not None:
                    flush_divide(pending, ctx, nc, ps_s, rden, zdiv, z_free,
                                 s_free, zdiv_done, ones_r2, ev_ones2, ev_outproj_box[0])
                ev_av = [None, None]
                for j in range(np_t):
                    if j + 1 < np_t:
                        ev_sc_l[j + 1] = emit_scores(j + 1)
                    ev_e_l[j] = emit_exp(j, ev_sc_l[j])
                    if j >= 1:
                        ev_av = emit_av(j - 1, ev_e_l[j - 1])
                ev_av = emit_av(np_t - 1, ev_e_l[np_t - 1])

                ctx.wait("s", ev_av[0])
                ctx.wait("s", ev_av[1])
                ctx.wait("s", pending[1])
                nc.scalar.activation(out=rden[:], in_=z_t[64:65, :], func=AF.Ln)
                nc.scalar.drain()
                nc.scalar.activation(out=rden[:], in_=rden[:], func=AF.Exp,
                                     scale=-1.0).then_inc(ctx.sems["S"], 1)
                ev_rd = ctx.inc("S", 1)
                pending[0] = (h0, z_t, ev_rd)
                if PARANOID:
                    ctx.barrier()
            if pending[0] is not None:
                flush_divide(pending, ctx, nc, ps_s, rden, zdiv, z_free,
                             s_free, zdiv_done, ones_r2, ev_ones2, ev_outproj_box[0])
            if "dbg_zd" in W and qb == 0:
                ctx.barrier()
                for h in range(NH):
                    nc.gpsimd.dma_start(out=W["dbg_zd"][h, :, :], in_=zdiv[h][:]
                                        ).then_inc(ctx.sems["DG"], 16)
                    ctx.inc("DG", 16)
                ctx.wait("g", ("DG", ctx.cnt["DG"]))
                ctx.barrier()
            for u in range(4):
                if qb < NTB - 1:
                    pend_op.append((qb, u))
                    continue
                i = 4 * qb + u
                ctx.wait("t", zdiv_done[NH - 1])
                ctx.wait("t", ev_wo)
                ctx.wait("t", ev_bo)
                ctx.wait("t", ps_y_free)
                for half in range(2):
                    for h in range(NH):
                        nc.tensor.matmul(ps_y[:, half * 512:(half + 1) * 512],
                                         zdiv[h][:, u * 128:(u + 1) * 128],
                                         woext[:, h * 1024 + half * 512:h * 1024 + (half + 1) * 512],
                                         start=(h == 0), stop=False, skip_group_check=True)
                    mm = nc.tensor.matmul(ps_y[:, half * 512:(half + 1) * 512], ones_r2[:],
                                          bo[:, half * 512:(half + 1) * 512],
                                          start=False, stop=True, skip_group_check=True)
                mm.then_inc(ctx.sems["T"], 1)
                ev_yo = ctx.inc("T", 1)
                ev_outproj_box[0] = ev_yo
                if "dbg_y" in W and i == 0:
                    ctx.wait("v", ev_yo)
                    nc.vector.tensor_copy(e_sb[0][:], ps_y[:]).then_inc(ctx.sems["V"], 1)
                    dbg_ev = ctx.inc("V", 1)
                    ctx.wait("g", dbg_ev)
                    nc.gpsimd.dma_start(out=W["dbg_y"][:, :], in_=e_sb[0][:]).then_inc(ctx.sems["DG"], 16)
                    dbg_ev2 = ctx.inc("DG", 16)
                    ctx.wait("s", dbg_ev2)
                ev_done, ps_y_free = emit_phase_c(ctx, sf, io["x_in"], io["x_out"],
                                                  ps_y, ev_yo, i, cbufs)
            qt_free = ("T", ctx.cnt["T"])
    return ev_done


def l0a_weight_maps(inputs):
    W_Q = np.asarray(inputs["L0_attn_W_Q"])
    W_K = np.asarray(inputs["L0_attn_W_K"])
    W_V = np.asarray(inputs["L0_attn_W_V"])
    W_O = np.asarray(inputs["L0_attn_W_O"])
    b_Q = np.asarray(inputs["L0_attn_b_Q"]).reshape(-1)
    b_K = np.asarray(inputs["L0_attn_b_K"]).reshape(-1)
    b_V = np.asarray(inputs["L0_attn_b_V"]).reshape(-1)
    b_O = np.asarray(inputs["L0_attn_b_O"])
    cat = lambda w: w.transpose(1, 0, 2).reshape(D, NCH * 128)
    wqkv = np.concatenate([cat(W_Q), cat(W_K), cat(W_V)], axis=1)
    bqk = np.stack([b_Q, b_K]).reshape(2, D, 1)
    woext = np.zeros((64, 16 * 1024), np.float32)
    for h in range(16):
        woext[:, h * 1024:(h + 1) * 1024] = W_O[h]
    return {
        "w_qkv": np.ascontiguousarray(wqkv, np.float32),
        "w_bqk": np.ascontiguousarray(bqk, np.float32),
        "w_bvrow": np.ascontiguousarray(b_V.reshape(1, -1), np.float32),
        "w_oext": woext,
        "w_borow": np.ascontiguousarray(b_O.reshape(1, -1), np.float32),
    }


def emit_l1_attn(ctx, st, io, W):
    nc = ctx.nc
    HN = 4
    xn_dim = [st.enter_context(nc.sbuf_tensor(f"a_xnd{c}", [128, 4096], BF16)) for c in range(NCH)]
    ident_bf = st.enter_context(nc.sbuf_tensor("a_id", [128, 128], BF16))
    wqkv = [st.enter_context(nc.sbuf_tensor(f"a_wq{c}", [128, 768], BF16)) for c in range(NCH)]
    woT = [st.enter_context(nc.sbuf_tensor(f"a_wo{j}", [128, 1024], BF16)) for j in range(2)]
    mask0 = st.enter_context(nc.sbuf_tensor("a_m0", [128, 128], BF16))
    onesr = st.enter_context(nc.sbuf_tensor("a_ones", [1, 128], F32))
    onesc = st.enter_context(nc.sbuf_tensor("a_onesc", [128, 1], BF16))
    k_tok = [st.enter_context(nc.sbuf_tensor(f"a_kk{i}", [128, 256], BF16)) for i in range(NT)]
    q_scr = [st.enter_context(nc.sbuf_tensor(f"a_qs{i}", [128, 256], BF16)) for i in range(2)]
    v_tok = [st.enter_context(nc.sbuf_tensor(f"a_v{i}", [128, 264], BF16)) for i in range(NT)]
    qT = [st.enter_context(nc.sbuf_tensor(f"a_qT{j}", [128, 4096], BF16)) for j in range(2)]
    kT = [st.enter_context(nc.sbuf_tensor(f"a_kT{j}", [128, 4096], BF16)) for j in range(2)]
    yT = [st.enter_context(nc.sbuf_tensor(f"a_yT{j}", [128, 4096], BF16)) for j in range(2)]
    kvk = st.enter_context(nc.sbuf_tensor("a_kvk", [128, 132], BF16))
    at_sb = [[st.enter_context(nc.sbuf_tensor(f"a_at{r}_{h}", [128, 128], BF16)) for h in range(HN)]
             for r in range(2)]
    rden = [st.enter_context(nc.sbuf_tensor(f"a_rden{i}", [1, 256], F32)) for i in range(2)]
    mtmp = [st.enter_context(nc.sbuf_tensor(f"a_mt{i}", [128, 512], F32)) for i in range(2)]
    psb = [st.enter_context(nc.sbuf_tensor(f"a_pp{i}", [128, 512], F32)) for i in range(2)]
    rn8 = st.enter_context(nc.sbuf_tensor("a_rn", [128, 8 * 2], F32))

    def gdma(dst, src):
        nc.gpsimd.dma_start(out=dst, in_=src).then_inc(ctx.sems["DG"], 16)
        return ctx.inc("DG", 16)

    ev_id = gdma(ident_bf[:], W["ident"][:, :])
    for c in range(NCH):
        ev_w = gdma(wqkv[c][:], W["wqkvT1"][c * 128:(c + 1) * 128, :])
    for j in range(2):
        ev_w = gdma(woT[j][:], W["woT"][j * 128:(j + 1) * 128, :])
    ev_mk0 = gdma(mask0[:], W["masks"][0, :, 0:128])
    nc.vector.memset(onesr[:], 1.0)
    nc.vector.memset(onesc[:], 1.0).then_inc(ctx.sems["V"], 1)
    ev_ones = ctx.inc("V", 1)

    pa = emit_phase_a(ctx, st, io["x_in"], xn_dim, ev_id, ident_bf)
    ctx.barrier()
    if "dbg_xnd" in W:
        nc.gpsimd.dma_start(out=W["dbg_xnd"][:, :], in_=xn_dim[0][:]).then_inc(ctx.sems["DG"], 16)
        ctx.inc("DG", 16)

    with contextlib.ExitStack() as ps1:
        ps_qkv = [ps1.enter_context(nc.psum_tensor(f"a_pq{i}", [128, 1024], F32)) for i in range(2)]
        pstr = [ps1.enter_context(nc.psum_tensor(f"a_ptr{i}", [128, 128], BF16)) for i in range(2)]
        qkv_free = [None, None]
        scr_free = [None, None]
        tr_free = [None, None]
        trq_free = [None, None]
        ev_qkv_l = [None] * NT

        def emit_qkv(i):
            r = i % 2
            ctx.wait("t", ev_w)
            ctx.wait("t", qkv_free[r])
            for seg, (lo, hi) in enumerate(((0, 512), (512, 768))):
                for kc in range(NCH):
                    mm = nc.tensor.matmul(ps_qkv[r][:, lo:hi],
                                          xn_dim[kc][:, i * 128:(i + 1) * 128],
                                          wqkv[kc][:, lo:hi], start=(kc == 0),
                                          stop=(kc == NCH - 1), skip_group_check=True)
            mm.then_inc(ctx.sems["T"], 1)
            return ctx.inc("T", 1)

        def emit_post(i):
            r = i % 2
            ev_qkv = ev_qkv_l[i]
            ctx.wait("v", ev_qkv)
            ctx.wait("v", scr_free[r])
            nc.vector.tensor_scalar(out=mtmp[r][:], in0=ps_qkv[r][:, 0:512],
                                    scalar1=0.0, scalar2=None, op0=ALU.min
                                    ).then_inc(ctx.sems["V"], 1)
            ev_min = ctx.inc("V", 1)
            ctx.wait("s", ev_min)
            nc.scalar.activation(out=mtmp[r][:], in_=mtmp[r][:], func=AF.Exp
                                 ).then_inc(ctx.sems["S"], 1)
            ev_exp = ctx.inc("S", 1)
            ctx.wait("v", ev_exp)
            nc.vector.scalar_tensor_tensor(out=psb[r][:], in0=ps_qkv[r][:, 0:512],
                                           scalar=0.0, in1=mtmp[r][:],
                                           op0=ALU.max, op1=ALU.add)
            nc.vector.drain()
            nc.vector.tensor_tensor(out=mtmp[r][:], in0=psb[r][:], in1=psb[r][:],
                                    op=ALU.mult)
            nc.vector.drain()
            nc.vector.tensor_reduce(rn8[:, 8 * r:8 * r + 8],
                                    mtmp[r][:].rearrange("p (h d) -> p h d", d=64),
                                    axis=AX.X, op=ALU.add).then_inc(ctx.sems["V"], 1)
            ev_s8 = ctx.inc("V", 1)
            ctx.wait("s", ev_s8)
            nc.scalar.activation(out=rn8[:, 8 * r:8 * r + 8], in_=rn8[:, 8 * r:8 * r + 8],
                                 func=AF.Ln)
            nc.scalar.drain()
            nc.scalar.activation(out=rn8[:, 8 * r:8 * r + 8], in_=rn8[:, 8 * r:8 * r + 8],
                                 func=AF.Exp, scale=0.5).then_inc(ctx.sems["S"], 1)
            ev_sqrt = ctx.inc("S", 1)
            ctx.wait("v", ev_sqrt)
            nc.vector.tensor_scalar(out=rn8[:, 8 * r:8 * r + 8], in0=rn8[:, 8 * r:8 * r + 8],
                                    scalar1=1e-6, scalar2=None, op0=ALU.add)
            nc.vector.drain()
            nc.vector.reciprocal(out=rn8[:, 8 * r:8 * r + 8], in_=rn8[:, 8 * r:8 * r + 8]
                                 ).then_inc(ctx.sems["V"], 1)
            ev_rn = ctx.inc("V", 1)
            ctx.wait("v", trq_free[r])
            for hh in range(8):
                dstt = q_scr[r] if hh < 4 else k_tok[i]
                dco = 64 * (hh % 4)
                gg = nc.vector.tensor_scalar(out=dstt[:, dco:dco + 64],
                                             in0=psb[r][:, 64 * hh:64 * hh + 64],
                                             scalar1=rn8[:, 8 * r + hh:8 * r + hh + 1],
                                             scalar2=None, op0=ALU.mult)
            gg.then_inc(ctx.sems["V"], 1)
            ev_qk = ctx.inc("V", 1)
            for h in range(4):
                nc.vector.tensor_copy(v_tok[i][:, 66 * h:66 * h + 64],
                                      ps_qkv[r][:, 512 + 64 * h:512 + 64 * h + 64])
            nc.vector.memset(v_tok[i][:].rearrange("p (h u) -> p h u", u=66)[:, :, 64:65], 1.0
                             ).then_inc(ctx.sems["V"], 1)
            ev_tok = ctx.inc("V", 1)
            qkv_free[r] = ev_tok
            scr_free[r] = [ev_tok, ev_qk]
            for c4 in range(4):
                pb = c4 % 2
                srct = q_scr[r] if c4 < 2 else k_tok[i]
                sco = 128 * (c4 % 2)
                ctx.wait("t", ev_qk)
                ctx.wait("t", tr_free[pb])
                nc.tensor.transpose(pstr[pb][:], srct[:, sco:sco + 128],
                                    ident_bf[:]).then_inc(ctx.sems["T"], 1)
                ev_tr = ctx.inc("T", 1)
                if c4 == 1:
                    trq_free[r] = ev_tr
                dst = (qT if c4 < 2 else kT)[c4 % 2]
                ctx.wait("v", ev_tr)
                nc.vector.tensor_copy(dst[:, i * 128:(i + 1) * 128], pstr[pb][:]
                                      ).then_inc(ctx.sems["V"], 1)
                tr_free[pb] = ctx.inc("V", 1)

        for i in range(NT):
            ev_qkv_l[i] = emit_qkv(i)
            if i >= 1:
                emit_post(i - 1)
        emit_post(NT - 1)
        ctx.barrier()
    if "dbg_qt" in W:
        nc.gpsimd.dma_start(out=W["dbg_qt"][:, :], in_=qT[0][:, 0:256]).then_inc(ctx.sems["DG"], 16)
        nc.gpsimd.dma_start(out=W["dbg_kt"][:, :], in_=kT[0][:, 0:256]).then_inc(ctx.sems["DG"], 16)
        nc.gpsimd.dma_start(out=W["dbg_vt"][:, :], in_=v_tok[0][:]).then_inc(ctx.sems["DG"], 16)
        nc.gpsimd.dma_start(out=W["dbg_ktok"][:, :], in_=k_tok[0][:]).then_inc(ctx.sems["DG"], 16)
        ctx.inc("DG", 64)

    with contextlib.ExitStack() as ps2:
        ps_kv = ps2.enter_context(nc.psum_tensor("a_pkv", [128, 132], F32))
        ps_a = [ps2.enter_context(nc.psum_tensor(f"a_pa{i}", [128, 384], F32)) for i in range(2)]
        ps_y = [ps2.enter_context(nc.psum_tensor(f"a_py{i}", [128, 128], F32)) for i in range(2)]
        ps_bc = ps2.enter_context(nc.psum_tensor("a_pbc", [128, 512], F32))

        nc.vector.memset(kvk[:], 0.0).then_inc(ctx.sems["V"], 1)
        ev_kvk_evac = ctx.inc("V", 1)
        mask_done = [None, None]
        y_free = None
        d_free = None
        bc_free = None
        ev_d_l = [None] * NT
        ev_dT_l = [None] * NT
        ev_y_l = [None] * NT

        def emit_scores_masks(c):
            cs = slice(c * 128, (c + 1) * 128)
            cr = c % 2
            for pair in range(2):
                ctx.wait("t", mask_done[0])
                ctx.wait("t", mask_done[1])
                if pair == 0:
                    ctx.wait("t", d_free)
                for p in range(2):
                    mm = nc.tensor.matmul(ps_a[p][:, 0:128], kT[pair][64 * p:64 * p + 64, cs],
                                          qT[pair][64 * p:64 * p + 64, cs],
                                          start=True, stop=True, skip_group_check=True,
                                          tile_position=(64 * p, 0))
                mm.then_inc(ctx.sems["T"], 1)
                ev_sc = ctx.inc("T", 1)
                ctx.wait("v", ev_sc)
                ctx.wait("v", ev_mk0)
                ctx.wait("v", ev_dT_l[c - 2] if c >= 2 else None)
                for p in range(2):
                    h = 2 * pair + p
                    last = nc.vector.tensor_tensor(out=at_sb[cr][h][:], in0=ps_a[p][:, 0:128],
                                                   in1=mask0[:], op=ALU.mult)
                last.then_inc(ctx.sems["V"], 1)
                mask_done[cr] = ctx.inc("V", 1)
            return mask_done[cr]

        ev_at_l = [None] * NT
        ev_at_l[0] = emit_scores_masks(0)
        for c in range(NT):
            cs = slice(c * 128, (c + 1) * 128)
            cr = c % 2
            ev_at = ev_at_l[c]
            ctx.wait("t", ev_at)
            ctx.wait("t", y_free)
            for h in range(HN):
                bank = h // 2
                b = 64 * (h % 2)
                nc.tensor.matmul(ps_y[bank][b:b + 64, 0:128],
                                 v_tok[c][:, 66 * h:66 * h + 64], at_sb[cr][h][:],
                                 start=True, stop=False, skip_group_check=True,
                                 tile_position=(0, b))
            ctx.wait("t", ev_kvk_evac)
            for h in range(HN):
                bank = h // 2
                b = 64 * (h % 2)
                j = h // 2
                mm = nc.tensor.matmul(ps_y[bank][b:b + 64, 0:128],
                                      kvk[b:b + 64, 66 * j:66 * j + 64],
                                      qT[j][b:b + 64, cs],
                                      start=False, stop=True, skip_group_check=True,
                                      tile_position=(b, b))
            mm.then_inc(ctx.sems["T"], 1)
            ev_y = ctx.inc("T", 1)
            ev_y_l[c] = ev_y

            for h in range(HN):
                dk = h % 2
                dco = 128 + 128 * (h // 2)
                nc.tensor.matmul(ps_a[dk][0:1, dco:dco + 128], onesc[:], at_sb[cr][h][:],
                                 start=False, stop=False, skip_group_check=True)
            for h in range(HN):
                b = 64 * (h % 2)
                j = h // 2
                dk = h % 2
                dco = 128 + 128 * (h // 2)
                mm = nc.tensor.matmul(ps_a[dk][0:1, dco:dco + 128],
                                      kvk[b:b + 64, 66 * j + 64:66 * j + 65],
                                      qT[j][b:b + 64, cs],
                                      start=False, stop=True, skip_group_check=True,
                                      tile_position=(b, 0))
            mm.then_inc(ctx.sems["T"], 1)
            ev_d = ctx.inc("T", 1)
            ev_dT_l[c] = ev_d

            for h in range(HN):
                b = 64 * (h % 2)
                j = h // 2
                mm = nc.tensor.matmul(ps_kv[b:b + 64, 66 * j:66 * j + 65],
                                      k_tok[c][:, 64 * h:64 * h + 64],
                                      v_tok[c][:, 66 * h:66 * h + 65],
                                      start=(c == 0 and h < 2), stop=(c == NT - 1),
                                      skip_group_check=True, tile_position=(0, b))
            mm.then_inc(ctx.sems["T"], 1)
            ev_upd = ctx.inc("T", 1)
            ctx.wait("v", ev_upd)
            nc.vector.tensor_copy(kvk[:], ps_kv[:]).then_inc(ctx.sems["V"], 1)
            ev_kvk_evac = ctx.inc("V", 1)

            if c + 1 < NT:
                ev_at_l[c + 1] = emit_scores_masks(c + 1)

            ctx.wait("v", ev_y)
            for j in range(2):
                nc.vector.tensor_copy(yT[j][:, cs], ps_y[j][:])
            nc.vector.drain()
            nc.vector.sem_inc(ctx.sems["V"], 1)
            y_free = ctx.inc("V", 1)
            ctx.wait("v", ev_d)
            ctx.wait("v", bc_free)
            for dk in range(2):
                nc.vector.tensor_scalar(out=rden[dk][:], in0=ps_a[dk][0:1, 128:384],
                                        scalar1=1e-6, scalar2=None, op0=ALU.add)
            nc.vector.drain()
            for dk in range(2):
                last = nc.vector.reciprocal(out=rden[dk][:], in_=rden[dk][:])
            last.then_inc(ctx.sems["V"], 1)
            ev_rden = ctx.inc("V", 1)
            ev_d_l[c] = ev_rden
            d_free = ev_rden
            ctx.wait("t", ev_rden)
            ctx.wait("t", ev_ones)
            ctx.wait("t", bc_free)
            for h in range(HN):
                dk = h % 2
                dco = 128 * (h // 2)
                mm = nc.tensor.matmul(ps_bc[:, 128 * h:128 * h + 128], onesr[:],
                                      rden[dk][0:1, dco:dco + 128],
                                      start=(h == 0), stop=(h == HN - 1),
                                      skip_group_check=True)
            mm.then_inc(ctx.sems["T"], 1)
            ev_bc = ctx.inc("T", 1)
            ctx.wait("v", ev_bc)
            ctx.wait("v", y_free)
            for h in range(HN):
                b = 64 * (h % 2)
                j = h // 2
                last = nc.vector.tensor_tensor(out=yT[j][b:b + 64, cs],
                                               in0=yT[j][b:b + 64, cs],
                                               in1=ps_bc[b:b + 64, 128 * h:128 * h + 128],
                                               op=ALU.mult)
            last.then_inc(ctx.sems["V"], 1)
            div_done = ctx.inc("V", 1)
            bc_free = div_done
        ctx.barrier()

    with contextlib.ExitStack() as ps3:
        ps_o = [ps3.enter_context(nc.psum_tensor(f"a_po{r}", [128, 1024], F32)) for r in range(2)]
        cbufs = mk_phase_c_bufs(ctx, ps3)
        ps_o_free = [None, None]
        ev_done = None
        for i in range(NT):
            yr = i % 2
            ctx.wait("t", ps_o_free[yr])
            for half in range(2):
                for j in range(2):
                    mm = nc.tensor.matmul(ps_o[yr][:, half * 512:(half + 1) * 512],
                                          yT[j][:, i * 128:(i + 1) * 128],
                                          woT[j][:, half * 512:(half + 1) * 512],
                                          start=(j == 0), stop=(j == 1),
                                          skip_group_check=True)
            mm.then_inc(ctx.sems["T"], 1)
            ev_yo = ctx.inc("T", 1)
            ev_done, yfree = emit_phase_c(ctx, st, io["x_in"], io["x_out"],
                                          ps_o[yr], ev_yo, i, cbufs)
            ps_o_free[yr] = yfree
    return ev_done


def l1a_weight_maps(inputs):
    Wq = np.asarray(inputs["L1_attn_Wq"])
    Wk = np.asarray(inputs["L1_attn_Wk"])
    Wv = np.asarray(inputs["L1_attn_Wv"])
    Wo = np.asarray(inputs["L1_attn_Wo"])
    wqkvT1 = np.concatenate([Wq.T, Wk.T, Wv.T], axis=1)
    return {
        "w_qkvT1": np.ascontiguousarray(wqkvT1, np.float32),
        "w_oT": np.ascontiguousarray(Wo.T, np.float32),
    }


def make_masks():
    m = np.zeros((4, 128, 512), np.float32)
    for j in range(4):
        off = 128 * j
        p = np.arange(128)[:, None]
        q = np.arange(512)[None, :]
        m[j] = (p + off <= q).astype(np.float32)
    return m


def make_maskb():
    return -1e6 * (1.0 - make_masks())


BLOCK_EMITTERS = {}


def emit_l1_ffn(ctx, st, io, W):
    nc = ctx.nc
    NE = 8
    xn_dim = [st.enter_context(nc.sbuf_tensor(f"m_xnd{c}", [128, 4096], BF16)) for c in range(NCH)]
    xn_tok = [st.enter_context(nc.sbuf_tensor(f"m_xnt{i}", [128, 1024], BF16)) for i in range(NT)]
    ident_bf = st.enter_context(nc.sbuf_tensor("m_id", [128, 128], BF16))
    wbase = [st.enter_context(nc.sbuf_tensor(f"m_wb{c}", [128, 1024], BF16)) for c in range(NCH)]
    bbase_r = st.enter_context(nc.sbuf_tensor("m_bb", [1, 1024], BF16))
    cols = st.enter_context(nc.sbuf_tensor("m_cols", [128, 16], F32))
    sx_sb = st.enter_context(nc.sbuf_tensor("m_sx", [128, 8], F32))
    sq_scr = st.enter_context(nc.sbuf_tensor("m_sq", [128, 8], F32))
    onesbf_r = st.enter_context(nc.sbuf_tensor("m_1r", [1, 128], BF16))
    onesbf_c = st.enter_context(nc.sbuf_tensor("m_1c", [128, 1], BF16))
    onesf_c = st.enter_context(nc.sbuf_tensor("m_1fc", [128, 1], F32))
    onesf_r = st.enter_context(nc.sbuf_tensor("m_1fr", [1, 128], F32))
    kxones = st.enter_context(nc.sbuf_tensor("m_kx1", [128, 1], F32))
    rows8 = st.enter_context(nc.sbuf_tensor("m_r8", [1, 40], F32))
    iota8 = st.enter_context(nc.sbuf_tensor("m_io", [1, 8], F32))
    ohc = st.enter_context(nc.sbuf_tensor("m_ohc", [128, 8], F32))
    ohcol = st.enter_context(nc.sbuf_tensor("m_ohl", [8, 1], F32))
    wadn_sel = [st.enter_context(nc.sbuf_tensor(f"m_was{c}", [128, 64], BF16)) for c in range(NCH)]
    waup_sel = st.enter_context(nc.sbuf_tensor("m_wus", [64, 1024], BF16))
    basel = st.enter_context(nc.sbuf_tensor("m_bas", [64, 1], F32))
    bupsel = st.enter_context(nc.sbuf_tensor("m_bus", [1, 1024], BF16))

    def gdma(dst, src):
        nc.gpsimd.dma_start(out=dst, in_=src).then_inc(ctx.sems["DG"], 16)
        return ctx.inc("DG", 16)

    ev_id = gdma(ident_bf[:], W["ident"][:, :])
    for c in range(NCH):
        gdma(wbase[c][:], W["wbase"][c * 128:(c + 1) * 128, :])
    gdma(bbase_r[:], W["bbase"][:, :])
    ev_wres = gdma(iota8[:], W["iota8"][:, :])
    nc.vector.memset(onesbf_r[:], 1.0)
    nc.vector.memset(onesbf_c[:], 1.0)
    nc.vector.memset(onesf_c[:], 1.0)
    nc.vector.memset(onesf_r[:], 1.0)
    nc.vector.memset(kxones[:], 1024.0)
    nc.vector.memset(cols[:], 0.0).then_inc(ctx.sems["V"], 1)
    ev_ones = ctx.inc("V", 1)

    stats = st.enter_context(nc.sbuf_tensor("m_st", [128, 4 * NT], F32))
    with contextlib.ExitStack() as pa_sc:
        pa = emit_phase_a(ctx, pa_sc, io["x_in"], xn_dim, ev_id, ident_bf,
                          xn_tok=xn_tok, stats_sb=stats, ring=4)
        ctx.barrier()

    with contextlib.ExitStack() as sp:
        ps_sm = sp.enter_context(nc.psum_tensor("m_psm", [1, 24], F32))
        bd2t = [sp.enter_context(nc.sbuf_tensor(f"m_b2t{c}", [128, 8], F32)) for c in range(NCH)]
        for c in range(NCH):
            ev_wres = gdma(bd2t[c][:], W["bd2t"][c * 128:(c + 1) * 128, :])
        ctx.wait("v", ev_wres)
        ctx.wait("t", ev_wres)
        for c in range(NCH):
            nc.vector.tensor_reduce(sx_sb[:, c:c + 1], xn_dim[c][:], axis=AX.X, op=ALU.add)
            nc.vector.drain()
        for i in range(NT):
            nc.vector.tensor_scalar(out=cols[:, 1:2], in0=stats[:, 4 * i + 1:4 * i + 2],
                                    scalar1=stats[:, 4 * i + 2:4 * i + 3],
                                    scalar2=stats[:, 4 * i + 2:4 * i + 3],
                                    op0=ALU.mult, op1=ALU.mult)
            nc.vector.drain()
            nc.vector.tensor_tensor(out=cols[:, 0:1], in0=cols[:, 0:1], in1=cols[:, 1:2],
                                    op=ALU.add)
            nc.vector.drain()
        nc.vector.sem_inc(ctx.sems["V"], 1)
        ev_kx = ctx.inc("V", 1)
        ev_sq_mm = None
        for c in range(NCH):
            ctx.wait("v", ev_sq_mm)
            nc.vector.tensor_tensor(out=sq_scr[:], in0=bd2t[c][:], in1=bd2t[c][:], op=ALU.mult
                                    ).then_inc(ctx.sems["V"], 1)
            ev_sq = ctx.inc("V", 1)
            ctx.wait("t", ev_sq)
            nc.tensor.matmul(ps_sm[:, 8:16], onesf_c[:], sq_scr[:],
                             start=(c == 0), stop=(c == NCH - 1), skip_group_check=True
                             ).then_inc(ctx.sems["T"], 1)
            ev_sq_mm = ctx.inc("T", 1)
        ctx.wait("t", ev_kx)
        for c in range(NCH):
            nc.tensor.matmul(ps_sm[:, 0:8], sx_sb[:, c:c + 1], bd2t[c][:],
                             start=(c == 0), stop=(c == NCH - 1), skip_group_check=True)
        nc.tensor.matmul(ps_sm[:, 16:17], kxones[:], cols[:, 0:1],
                         start=True, stop=True, skip_group_check=True).then_inc(ctx.sems["T"], 1)
        ev_dots = ctx.inc("T", 1)
        ctx.wait("v", ev_dots)
        nc.vector.tensor_scalar(out=rows8[:, 8:16], in0=ps_sm[:, 8:16], scalar1=4096.0,
                                scalar2=None, op0=ALU.mult)
        nc.vector.tensor_scalar(out=rows8[:, 16:24], in0=ps_sm[:, 0:8], scalar1=-2.0,
                                scalar2=None, op0=ALU.mult)
        nc.vector.drain()
        nc.vector.tensor_tensor(out=rows8[:, 0:8], in0=rows8[:, 8:16], in1=rows8[:, 16:24],
                                op=ALU.add)
        nc.vector.drain()
        nc.vector.tensor_scalar(out=rows8[:, 0:8], in0=rows8[:, 0:8],
                                scalar1=ps_sm[0:1, 16:17], scalar2=None, op0=ALU.add
                                ).then_inc(ctx.sems["V"], 1)
        ev_crow = ctx.inc("V", 1)
        ctx.barrier()

    with contextlib.ExitStack() as se:
        wadn_all = [se.enter_context(nc.sbuf_tensor(f"m_wad{c}", [128, 512], BF16)) for c in range(NCH)]
        waup_all = se.enter_context(nc.sbuf_tensor("m_wau", [64, 8192], BF16))
        badn_all = se.enter_context(nc.sbuf_tensor("m_bad", [8, 64], F32))
        baup_all = se.enter_context(nc.sbuf_tensor("m_bau", [8, 1024], F32))
        wd1 = [[se.enter_context(nc.sbuf_tensor(f"m_w1{b}_{c}", [128, 256], BF16)) for c in range(NCH)]
               for b in range(2)]
        wd2 = [se.enter_context(nc.sbuf_tensor(f"m_w2{j}", [128, 1024], F32)) for j in range(2)]
        bd1r = [se.enter_context(nc.sbuf_tensor(f"m_b1r{b}", [1, 256], BF16)) for b in range(2)]
        bd2r = se.enter_context(nc.sbuf_tensor("m_b2r", [1, 1024], BF16))
        h_tok = [se.enter_context(nc.sbuf_tensor(f"m_h{i}", [128, 256], BF16)) for i in range(2)]
        H_sb = [se.enter_context(nc.sbuf_tensor(f"m_H{j}", [128, 256], F32)) for j in range(2)]
        nsh = se.enter_context(nc.sbuf_tensor("m_nsh", [1, 256], BF16))
        uprod = se.enter_context(nc.sbuf_tensor("m_up", [128, 256], F32))
        for c in range(NCH):
            gdma(wadn_all[c][:], W["wadn"][c * 128:(c + 1) * 128, :])
        gdma(waup_all[:], W["waup"][:, :])
        gdma(badn_all[:], W["badn"][:, :])
        ev_wres = gdma(baup_all[:], W["baup"][:, :])
        ps_h = [se.enter_context(nc.psum_tensor(f"m_ph{r}", [128, 256], F32)) for r in range(2)]
        ps_H = [se.enter_context(nc.psum_tensor(f"m_pH{j}", [128, 512], F32)) for j in range(2)]
        ps_M = [se.enter_context(nc.psum_tensor(f"m_pM{j}", [128, 512], F32)) for j in range(4)]

        ev_wd_ready = None
        ps_h_free = [None, None]
        ev_hs_l = [None] * NT
        h_read = [None, None]
        ev_Hev = None
        ev_loss_mm = None
        ev_wd1_l = [None] * (NE + 1)
        wd1_read = [None, None]
        for c in range(NCH):
            gdma(wd1[0][c][:], W["wd1"][0, c * 128:(c + 1) * 128, :])
        gdma(bd1r[0][:], W["bd1"][0:1, :])
        ev_wd1_l[0] = ctx.inc("DG", 0)
        for e in range(NE):
            eb = e % 2
            for j in range(2):
                gdma(wd2[j][:], W["wd2"][e, j * 128:(j + 1) * 128, :])
            ev_wd = gdma(bd2r[:], W["bd2"][e:e + 1, :])
            if e + 1 < NE:
                ctx.wait("g", wd1_read[(e + 1) % 2])
                for c in range(NCH):
                    gdma(wd1[(e + 1) % 2][c][:], W["wd1"][e + 1, c * 128:(c + 1) * 128, :])
                gdma(bd1r[(e + 1) % 2][:], W["bd1"][e + 1:e + 2, :])
                ev_wd1_l[e + 1] = ctx.inc("DG", 0)

            ev_m = None

            def emit_h(i):
                r = i % 2
                ctx.wait("t", ev_wd1_l[e])
                ctx.wait("t", ev_ones)
                ctx.wait("t", ps_h_free[r])
                for kc in range(NCH):
                    nc.tensor.matmul(ps_h[r][:], xn_dim[kc][:, i * 128:(i + 1) * 128],
                                     wd1[eb][kc][:], start=(kc == 0), stop=False,
                                     skip_group_check=True)
                nc.tensor.matmul(ps_h[r][:], onesbf_r[:], bd1r[eb][:], start=False, stop=True,
                                 skip_group_check=True).then_inc(ctx.sems["T"], 1)
                return ctx.inc("T", 1)

            def emit_gms(i, ev_hp):
                nonlocal ev_m
                r = i % 2
                sl = i % 2
                ctx.wait("s", ev_hp)
                ctx.wait("s", h_read[sl])
                nc.scalar.activation(out=h_tok[sl][:], in_=ps_h[r][:], func=AF.Relu
                                     ).then_inc(ctx.sems["S"], 1)
                ev_hs = ctx.inc("S", 1)
                ps_h_free[r] = ev_hs
                ctx.wait("t", ev_hs)
                for mh in range(2):
                    nc.tensor.matmul(ps_H[mh][:, 0:256], h_tok[sl][:, mh * 128:(mh + 1) * 128],
                                     h_tok[sl][:], start=(i == 0), stop=(i == NT - 1),
                                     skip_group_check=True)
                for mh in range(2):
                    for nh2 in range(2):
                        nc.tensor.matmul(ps_M[2 * mh + nh2][:],
                                         h_tok[sl][:, mh * 128:(mh + 1) * 128],
                                         xn_tok[i][:, nh2 * 512:(nh2 + 1) * 512],
                                         start=(i == 0), stop=False, skip_group_check=True)
                nc.tensor.matmul(ps_H[0][0:1, 256:512], onesbf_c[:], h_tok[sl][:],
                                 start=False, stop=False,
                                 skip_group_check=True).then_inc(ctx.sems["T"], 1)
                ev_m = ctx.inc("T", 1)
                h_read[sl] = ev_m

            for i in range(NT):
                ev_hs_l[i] = emit_h(i)
                if i >= 1:
                    emit_gms(i - 1, ev_hs_l[i - 1])
            emit_gms(NT - 1, ev_hs_l[NT - 1])
            ctx.wait("v", ev_m)
            nc.vector.tensor_scalar(out=nsh[:], in0=ps_H[0][0:1, 256:512], scalar1=-1.0,
                                    scalar2=None, op0=ALU.mult)
            for mh in range(2):
                nc.vector.tensor_copy(H_sb[mh][:], ps_H[mh][:, 0:256])
            nc.vector.drain()
            nc.vector.sem_inc(ctx.sems["V"], 1)
            ev_Hev = ctx.inc("V", 1)
            ctx.wait("t", ev_Hev)
            for mh in range(2):
                for nh2 in range(2):
                    mmu = nc.tensor.matmul(ps_M[2 * mh + nh2][:], nsh[:, mh * 128:(mh + 1) * 128],
                                           bd2r[:, nh2 * 512:(nh2 + 1) * 512], start=False,
                                           stop=True, skip_group_check=True)
            mmu.then_inc(ctx.sems["T"], 1)
            ev_U = ctx.inc("T", 1)
            ctx.wait("v", ev_U)
            for b8 in range(8):
                b4, pc = b8 // 2, b8 % 2
                mh, nh2 = b4 // 2, b4 % 2
                nc.vector.tensor_tensor(out=uprod[:], in0=ps_M[b4][:, pc * 256:(pc + 1) * 256],
                                        in1=wd2[mh][:, nh2 * 512 + pc * 256:nh2 * 512 + (pc + 1) * 256],
                                        op=ALU.mult)
                nc.vector.drain()
                nc.vector.tensor_reduce(cols[:, 2 + b8:3 + b8], uprod[:], axis=AX.X, op=ALU.add)
                nc.vector.drain()
            nc.vector.tensor_reduce(cols[:, 10:11], cols[:, 2:10], axis=AX.X, op=ALU.add)
            nc.vector.drain()
            nc.vector.tensor_scalar(out=cols[:, 10:11], in0=cols[:, 10:11], scalar1=-2.0,
                                    scalar2=None, op0=ALU.mult).then_inc(ctx.sems["V"], 1)
            ev_ucol = ctx.inc("V", 1)
            ctx.wait("t", ev_ucol)
            nc.tensor.matmul(ps_H[1][0:1, 256 + e:257 + e], onesf_c[:], cols[:, 10:11],
                             start=False, stop=False, skip_group_check=True)
            for mh in range(2):
                for nh2 in range(2):
                    for jc in range(2):
                        mmh = nc.tensor.matmul(ps_M[2 * mh + nh2][:],
                                               H_sb[jc][:, mh * 128:(mh + 1) * 128],
                                               wd2[jc][:, nh2 * 512:(nh2 + 1) * 512],
                                               start=(jc == 0), stop=(jc == 1),
                                               skip_group_check=True)
            mmh.then_inc(ctx.sems["T"], 1)
            ev_HW = ctx.inc("T", 1)
            ctx.wait("v", ev_HW)
            for b8 in range(8):
                b4, pc = b8 // 2, b8 % 2
                mh, nh2 = b4 // 2, b4 % 2
                nc.vector.tensor_tensor(out=uprod[:], in0=ps_M[b4][:, pc * 256:(pc + 1) * 256],
                                        in1=wd2[mh][:, nh2 * 512 + pc * 256:nh2 * 512 + (pc + 1) * 256],
                                        op=ALU.mult)
                nc.vector.drain()
                nc.vector.tensor_reduce(cols[:, 2 + b8:3 + b8], uprod[:], axis=AX.X, op=ALU.add)
                nc.vector.drain()
            nc.vector.tensor_reduce(cols[:, 10:11], cols[:, 2:10], axis=AX.X, op=ALU.add
                                    ).then_inc(ctx.sems["V"], 1)
            ev_acol = ctx.inc("V", 1)
            ctx.wait("t", ev_acol)
            nc.tensor.matmul(ps_H[1][0:1, 256 + e:257 + e], onesf_c[:], cols[:, 10:11],
                             start=False, stop=(e == NE - 1), skip_group_check=True
                             ).then_inc(ctx.sems["T"], 1)
            ev_loss_mm = ctx.inc("T", 1)
            wd1_read[eb] = ev_m
            ctx.wait("g", ev_acol)
            ctx.wait("g", ev_loss_mm)

        ctx.wait("v", ev_loss_mm)
        ctx.wait("v", ev_crow)
        nc.vector.tensor_tensor(out=rows8[:, 0:8], in0=rows8[:, 0:8],
                                in1=ps_H[1][0:1, 256:264], op=ALU.add)
        nc.vector.drain()
        nc.vector.tensor_reduce(rows8[:, 32:33], rows8[:, 0:8], axis=AX.X, op=ALU.min)
        nc.vector.drain()
        nc.vector.tensor_scalar(out=rows8[:, 8:16], in0=rows8[:, 0:8],
                                scalar1=rows8[:, 32:33], scalar2=None, op0=ALU.is_equal)
        nc.vector.drain()
        nc.vector.tensor_scalar(out=rows8[:, 8:16], in0=rows8[:, 8:16], scalar1=-1e9,
                                scalar2=1e9, op0=ALU.mult, op1=ALU.add)
        nc.vector.drain()
        ctx.wait("v", ev_wres)
        nc.vector.tensor_tensor(out=rows8[:, 8:16], in0=rows8[:, 8:16], in1=iota8[:], op=ALU.add)
        nc.vector.drain()
        nc.vector.tensor_reduce(rows8[:, 33:34], rows8[:, 8:16], axis=AX.X, op=ALU.min)
        nc.vector.drain()
        nc.vector.tensor_scalar(out=rows8[:, 16:24], in0=iota8[:], scalar1=rows8[:, 33:34],
                                scalar2=None, op0=ALU.is_equal).then_inc(ctx.sems["V"], 1)
        ev_oh = ctx.inc("V", 1)
        ctx.wait("t", ev_oh)
        ctx.wait("t", ev_Hev)
        nc.tensor.matmul(ps_H[0][:, 0:8], onesf_r[:], rows8[:, 16:24],
                         start=True, stop=True, skip_group_check=True)
        nc.tensor.transpose(ps_H[1][0:8, 0:1], rows8[:, 16:24], onesf_c[0:1, 0:1]
                            ).then_inc(ctx.sems["T"], 1)
        ev_bc = ctx.inc("T", 1)
        ctx.wait("v", ev_bc)
        nc.vector.tensor_copy(ohc[:], ps_H[0][:, 0:8])
        nc.vector.tensor_copy(ohcol[:], ps_H[1][0:8, 0:1])
        nc.vector.drain()
        nc.vector.sem_inc(ctx.sems["V"], 1)
        ev_ohc = ctx.inc("V", 1)
        ctx.wait("t", ev_ohc)
        nc.tensor.matmul(ps_H[0][0:64, 16:17], badn_all[:], ohcol[:],
                         start=True, stop=True, skip_group_check=True)
        mmb = None
        for half in range(2):
            mmb = nc.tensor.matmul(ps_M[0][0:1, 0:512] if half == 0 else ps_M[1][0:1, 0:512],
                                   ohcol[:], baup_all[:, half * 512:(half + 1) * 512],
                                   start=True, stop=True, skip_group_check=True)
        mmb.then_inc(ctx.sems["T"], 1)
        ev_bsel = ctx.inc("T", 1)
        ctx.wait("v", ev_bsel)
        nc.vector.tensor_copy(basel[:], ps_H[0][0:64, 16:17])
        nc.vector.tensor_copy(bupsel[:, 0:512], ps_M[0][0:1, 0:512])
        nc.vector.tensor_copy(bupsel[:, 512:1024], ps_M[1][0:1, 0:512])
        nc.vector.drain()
        ctx.wait("v", ev_wres)
        for c in range(NCH):
            nc.vector.tensor_scalar(out=wadn_sel[c][:], in0=wadn_all[c][:, 0:64],
                                    scalar1=ohc[:, 0:1], scalar2=None, op0=ALU.mult)
            nc.vector.drain()
            for e in range(1, NE):
                nc.vector.scalar_tensor_tensor(out=wadn_sel[c][:],
                                               in0=wadn_all[c][:, e * 64:(e + 1) * 64],
                                               scalar=ohc[:, e:e + 1], in1=wadn_sel[c][:],
                                               op0=ALU.mult, op1=ALU.add)
                nc.vector.drain()
        nc.vector.tensor_scalar(out=waup_sel[:], in0=waup_all[:, 0:1024],
                                scalar1=ohc[0:64, 0:1], scalar2=None, op0=ALU.mult)
        nc.vector.drain()
        for e in range(1, NE):
            nc.vector.scalar_tensor_tensor(out=waup_sel[:],
                                           in0=waup_all[:, e * 1024:(e + 1) * 1024],
                                           scalar=ohc[0:64, e:e + 1], in1=waup_sel[:],
                                           op0=ALU.mult, op1=ALU.add)
            nc.vector.drain()
        nc.vector.sem_inc(ctx.sems["V"], 1)
        ev_sel = ctx.inc("V", 1)
        ctx.barrier()

    with contextlib.ExitStack() as sfin:
        aT = sfin.enter_context(nc.sbuf_tensor("m_aT", [64, 4096], BF16))
        ps_a = sfin.enter_context(nc.psum_tensor("m_pa", [64, 512], F32))
        ps_y = [sfin.enter_context(nc.psum_tensor(f"m_py{r}", [128, 1024], F32)) for r in range(2)]
        cbufs = mk_phase_c_bufs(ctx, sfin)
        ps_a_free = None
        ps_y_free = [None, None]
        ev_done = None
        ctx.wait("t", ev_wres)
        ev_aT_tb = [None] * NTB
        for tb in range(NTB):
            tbs = slice(tb * 512, (tb + 1) * 512)
            ctx.wait("t", ps_a_free)
            for kc in range(NCH):
                mma = nc.tensor.matmul(ps_a[:], wadn_sel[kc][:], xn_dim[kc][:, tbs],
                                       start=(kc == 0), stop=(kc == NCH - 1),
                                       skip_group_check=True)
            mma.then_inc(ctx.sems["T"], 1)
            ev_ap = ctx.inc("T", 1)
            ctx.wait("s", ev_ap)
            nc.scalar.activation(out=aT[:, tbs], in_=ps_a[:], func=AF.Relu,
                                 bias=basel[:], scale=1.0).then_inc(ctx.sems["S"], 1)
            ps_a_free = ctx.inc("S", 1)
            ev_aT_tb[tb] = ps_a_free
        for i in range(NT):
            yr = i % 2
            ctx.wait("t", ev_aT_tb[i // 4])
            ctx.wait("t", ps_y_free[yr])
            for half in range(2):
                hs = slice(half * 512, (half + 1) * 512)
                for kc in range(NCH):
                    nc.tensor.matmul(ps_y[yr][:, hs], xn_dim[kc][:, i * 128:(i + 1) * 128],
                                     wbase[kc][:, hs], start=(kc == 0), stop=False,
                                     skip_group_check=True)
                nc.tensor.matmul(ps_y[yr][:, hs], onesbf_r[:], bbase_r[:, hs],
                                 start=False, stop=False, skip_group_check=True)
                nc.tensor.matmul(ps_y[yr][:, hs], aT[:, i * 128:(i + 1) * 128],
                                 waup_sel[:, hs], start=False, stop=False,
                                 skip_group_check=True)
                mmy = nc.tensor.matmul(ps_y[yr][:, hs], onesbf_r[:], bupsel[:, hs],
                                       start=False, stop=True, skip_group_check=True)
            mmy.then_inc(ctx.sems["T"], 1)
            ev_y = ctx.inc("T", 1)
            ev_done, yfree = emit_phase_c(ctx, sfin, io["x_in"], io["x_out"],
                                          ps_y[yr], ev_y, i, cbufs)
            ps_y_free[yr] = yfree
    return ev_done


def l1f_weight_maps(inputs):
    W_base = np.asarray(inputs["L1_ffn_W_base"])
    b_base = np.asarray(inputs["L1_ffn_b_base"])
    Wd1 = np.asarray(inputs["L1_ffn_Wd1"])
    bd1 = np.asarray(inputs["L1_ffn_bd1"])
    Wd2 = np.asarray(inputs["L1_ffn_Wd2"])
    bd2 = np.asarray(inputs["L1_ffn_bd2"])
    Wa_down = np.asarray(inputs["L1_ffn_Wa_down"])
    ba_down = np.asarray(inputs["L1_ffn_ba_down"])
    Wa_up = np.asarray(inputs["L1_ffn_Wa_up"])
    ba_up = np.asarray(inputs["L1_ffn_ba_up"])
    wadn = Wa_down.transpose(1, 0, 2).reshape(D, 8 * 64)
    waup = Wa_up.transpose(1, 0, 2).reshape(64, 8 * 1024)
    return {
        "w_base2": np.ascontiguousarray(W_base, np.float32),
        "w_bbase": np.ascontiguousarray(b_base.reshape(1, -1), np.float32),
        "w_d1": np.ascontiguousarray(Wd1, np.float32),
        "w_bd1": np.ascontiguousarray(bd1, np.float32),
        "w_d2": np.ascontiguousarray(Wd2, np.float32),
        "w_bd2": np.ascontiguousarray(bd2, np.float32),
        "w_bd2t": np.ascontiguousarray(bd2.T, np.float32),
        "w_adn": np.ascontiguousarray(wadn, np.float32),
        "w_badn": np.ascontiguousarray(ba_down, np.float32),
        "w_aup": np.ascontiguousarray(waup, np.float32),
        "w_baup": np.ascontiguousarray(ba_up, np.float32),
        "w_iota8": np.arange(8, dtype=np.float32).reshape(1, 8),
    }


DEBUG_TAPS = []


def build_program(blocks):
    nc = bass.Bass("TRN2", target_bir_lowering=False, debug=False)
    x_in0 = nc.dram_tensor("xin", [SEQ, D], F32, kind="ExternalInput").ap()
    x_out_f = nc.dram_tensor("xout", [SEQ, D], F32, kind="ExternalOutput").ap()
    ident = nc.dram_tensor("ident", [128, 128], F32, kind="ExternalInput").ap()
    scratch = [nc.dram_tensor(f"xs{i}", [SEQ, D], F32, kind="Internal").ap()
               for i in range(2)]

    W = {"ident": ident}
    W["masks"] = nc.dram_tensor("masks", [4, 128, 512], F32, kind="ExternalInput").ap()
    W["maskb"] = nc.dram_tensor("maskb", [4, 128, 512], F32, kind="ExternalInput").ap()
    for nm, shp in DEBUG_TAPS:
        W[nm] = nc.dram_tensor(nm, shp, F32, kind="ExternalOutput").ap()
    if "l0a" in blocks:
        W["wqkv"] = nc.dram_tensor("w_qkv", [D, 3072], F32, kind="ExternalInput").ap()
        W["bqk"] = nc.dram_tensor("w_bqk", [2, D, 1], F32, kind="ExternalInput").ap()
        W["bvrow"] = nc.dram_tensor("w_bvrow", [1, D], F32, kind="ExternalInput").ap()
        W["woext"] = nc.dram_tensor("w_oext", [64, 16 * 1024], F32, kind="ExternalInput").ap()
        W["borow"] = nc.dram_tensor("w_borow", [1, D], F32, kind="ExternalInput").ap()
        W["qt_s"] = nc.dram_tensor("qt_s", [D, SEQ], BF16, kind="Internal").ap()
        W["kt_s"] = nc.dram_tensor("kt_s", [D, SEQ], BF16, kind="Internal").ap()
        W["v_s"] = nc.dram_tensor("v_s", [NT, 128, 1040], BF16, kind="Internal").ap()
    if "l1a" in blocks:
        W["wqkvT1"] = nc.dram_tensor("w_qkvT1", [D, 768], F32, kind="ExternalInput").ap()
        W["woT"] = nc.dram_tensor("w_oT", [256, D], F32, kind="ExternalInput").ap()
    if "l1f" in blocks:
        W["wbase"] = nc.dram_tensor("w_base2", [D, D], F32, kind="ExternalInput").ap()
        W["bbase"] = nc.dram_tensor("w_bbase", [1, D], F32, kind="ExternalInput").ap()
        W["wd1"] = nc.dram_tensor("w_d1", [8, D, 256], F32, kind="ExternalInput").ap()
        W["bd1"] = nc.dram_tensor("w_bd1", [8, 256], F32, kind="ExternalInput").ap()
        W["wd2"] = nc.dram_tensor("w_d2", [8, 256, D], F32, kind="ExternalInput").ap()
        W["bd2"] = nc.dram_tensor("w_bd2", [8, D], F32, kind="ExternalInput").ap()
        W["bd2t"] = nc.dram_tensor("w_bd2t", [D, 8], F32, kind="ExternalInput").ap()
        W["wadn"] = nc.dram_tensor("w_adn", [D, 512], F32, kind="ExternalInput").ap()
        W["badn"] = nc.dram_tensor("w_badn", [8, 64], F32, kind="ExternalInput").ap()
        W["waup"] = nc.dram_tensor("w_aup", [64, 8192], F32, kind="ExternalInput").ap()
        W["baup"] = nc.dram_tensor("w_baup", [8, D], F32, kind="ExternalInput").ap()
        W["iota8"] = nc.dram_tensor("w_iota8", [1, 8], F32, kind="ExternalInput").ap()
    if "l0f" in blocks:
        W["winrep"] = nc.dram_tensor("w_winrep", [D, D], F32, kind="ExternalInput").ap()
        W["binrep"] = nc.dram_tensor("w_binrep", [D, 1], F32, kind="ExternalInput").ap()
        W["w1f"] = nc.dram_tensor("w_w1f", [D, 1], F32, kind="ExternalInput").ap()
        W["b1f"] = nc.dram_tensor("w_b1f", [D, 1], F32, kind="ExternalInput").ap()
        W["w2bd"] = nc.dram_tensor("w_w2bd", [NCH, 128, 64], F32, kind="ExternalInput").ap()
        W["b2"] = nc.dram_tensor("w_b2", [64, 1], F32, kind="ExternalInput").ap()
        W["wout"] = nc.dram_tensor("w_wout", [64, D], F32, kind="ExternalInput").ap()
        W["boutrow"] = nc.dram_tensor("w_bout", [1, D], F32, kind="ExternalInput").ap()

    orig_sb = nc.sbuf_tensor
    orig_ps = nc.psum_tensor
    uid = {"n": 0}

    def _sb(name, shape, dtype, **kw):
        return orig_sb(f"{name}_u{uid['n']}", shape, dtype, **kw)

    def _ps(name, shape, dtype, **kw):
        return orig_ps(f"{name}_u{uid['n']}", shape, dtype, **kw)

    nc.sbuf_tensor = _sb
    nc.psum_tensor = _ps

    with contextlib.ExitStack() as stack:
        ctx = Ctx(nc, stack)
        ctx.setup_consts(stack)
        n = len(blocks)
        for bi, b in enumerate(blocks):
            uid["n"] = bi + 1
            x_in = x_in0 if bi == 0 else scratch[(bi - 1) % 2]
            x_out = x_out_f if bi == n - 1 else scratch[bi % 2]
            io = {"x_in": x_in, "x_out": x_out}
            with contextlib.ExitStack() as bst:
                if b == "l0f":
                    emit_l0_ffn(ctx, bst, io, W)
                elif b == "l0a":
                    emit_l0_attn(ctx, bst, io, W)
                elif b == "l1a":
                    emit_l1_attn(ctx, bst, io, W)
                elif b == "l1f":
                    emit_l1_ffn(ctx, bst, io, W)
                else:
                    BLOCK_EMITTERS[b](ctx, bst, io, W)
                ctx.barrier()
    return nc


def l0f_weight_maps(inputs):
    Win = np.asarray(inputs["L0_ffn_Win"])
    bin_ = np.asarray(inputs["L0_ffn_bin_"])
    W1 = np.asarray(inputs["L0_ffn_W1"])
    b1 = np.asarray(inputs["L0_ffn_b1"])
    W2 = np.asarray(inputs["L0_ffn_W2"])
    b2 = np.asarray(inputs["L0_ffn_b2"])
    Wout = np.asarray(inputs["L0_ffn_Wout"])
    bout = np.asarray(inputs["L0_ffn_bout"])
    winrep = np.repeat(Win, 16, axis=1)
    binrep = np.repeat(bin_, 16)
    w1f = W1.reshape(-1)
    b1f = b1.reshape(-1)
    w2bd = np.zeros((NCH, 128, 64), np.float32)
    mm = np.arange(1024) // 16
    ii = np.arange(1024) % 16
    w2bd[np.arange(1024) // 128, np.arange(1024) % 128, mm] = W2[mm, ii]
    return {
        "w_winrep": np.ascontiguousarray(winrep, np.float32),
        "w_binrep": np.ascontiguousarray(binrep.reshape(-1, 1), np.float32),
        "w_w1f": np.ascontiguousarray(w1f.reshape(-1, 1), np.float32),
        "w_b1f": np.ascontiguousarray(b1f.reshape(-1, 1), np.float32),
        "w_w2bd": w2bd,
        "w_b2": np.ascontiguousarray(b2.reshape(-1, 1), np.float32),
        "w_wout": np.ascontiguousarray(Wout, np.float32),
        "w_bout": np.ascontiguousarray(bout.reshape(1, -1), np.float32),
    }


ALL_BLOCKS = ("l0a", "l0f", "l1a", "l1f")


def kernel(**inputs):
    x = np.asarray(inputs["L0_attn_x"], dtype=np.float32)
    wm = {}
    wm.update(l0a_weight_maps(inputs))
    wm.update(l0f_weight_maps(inputs))
    wm.update(l1a_weight_maps(inputs))
    wm.update(l1f_weight_maps(inputs))
    consts = {"ident": np.eye(128, dtype=np.float32), "masks": make_masks(),
              "maskb": make_maskb()}
    nc = build_program(ALL_BLOCKS)
    in_maps = [{"xin": np.ascontiguousarray(x[b]), **consts, **wm} for b in range(8)]
    res = run_bass_kernel_spmd(nc, in_maps, core_ids=list(range(8)))
    return np.stack([np.asarray(r["xout"], dtype=np.float32) for r in res.results])
